# version 18
# baseline (speedup 1.0000x reference)
import numpy as np
import ml_dtypes

import concourse.bass as bass
import concourse.mybir as mybir
from concourse.bass_utils import run_bass_kernel_spmd

F32 = mybir.dt.float32
BF16 = mybir.dt.bfloat16
FP8 = mybir.dt.float8e4
ALU = mybir.AluOpType
ACTF = mybir.ActivationFunctionType
AX = mybir.AxisListType

NPBF = ml_dtypes.bfloat16

TOK = 1024
NCH = 8
CH = 128
HID = 4096
NF = 32
H = 32
KVH = 8
D = 128
B, L = 2, 4096


class Sem:
    def __init__(self, h):
        self.h = h
        self.n = 0

    def inc(self, instr, k=1):
        instr.then_inc(self.h, k)
        self.n += k
        return self.n


def build_nc():
    nc = bass.Bass("TRN2", target_bir_lowering=False, debug=False)

    dram = {}
    for name, shape, dt in [
        ("x", [TOK, HID], F32),
        ("wqb", [H * HID, D], F32),
        ("wkb", [KVH * HID, D], F32),
        ("wvb", [KVH * HID, D], F32),
        ("wgb", [H * HID, D], F32),
        ("winT", [HID, 32], F32),
        ("wob", [32 * HID, D], F32),
        ("fwT", [HID, HID], F32),
        ("fwsl", [512, HID], F32),
        ("dtb", [32, 1], F32),
        ("alog", [32, 1], F32),
        ("gnormc", [D, 1], F32),
        ("fbias", [D, 32], F32),
        ("cosfm", [D, TOK], BF16),
        ("sin2fm", [D, TOK], BF16),
        ("mskd", [CH, 4], F32),
        ("ommd", [CH, 4], F32),
        ("tri", [CH, CH], F32),
        ("ones128", [CH, CH], F32),
        ("ones1", [1, CH], F32),
        ("rot", [CH, CH], BF16),
        ("id16", [CH, CH], BF16),
        ("id32", [CH, CH], F32),
        ("id8", [CH, CH], FP8),
        ("maskneg", [CH, CH], F32),
        ("negones1", [1, CH], F32),
    ]:
        dram[name] = nc.dram_tensor(name, shape, dt, kind="ExternalInput").ap()

    y_out = nc.dram_tensor("y", [TOK, HID], F32, kind="ExternalOutput").ap()

    kvscr = nc.dram_tensor("kvscr", [KVH * 3 * CH, TOK], BF16).ap()
    ygscr = nc.dram_tensor("ygscr", [H * CH, TOK], BF16).ap()
    ag_in = nc.dram_tensor("ag_in", [CH, 4128], F32).ap()
    ag_out = nc.dram_tensor("ag_out", [4 * CH, 4128], F32).ap()
    am_in = nc.dram_tensor("am_in", [1, 2], F32).ap()
    am_out = nc.dram_tensor("am_out", [1, 2], F32).ap()

    LIVE = {}

    def salloc(name, shape, dt):
        cm = nc.sbuf_tensor(name, shape, dt)
        t = cm.__enter__()
        LIVE[name] = cm
        return t

    def sfree(*names):
        for nm in names:
            LIVE.pop(nm).__exit__(None, None, None)

    tri = salloc("triS", [CH, CH], F32)
    ones128 = salloc("ones128S", [CH, CH], F32)
    ones1 = salloc("ones1S", [1, CH], F32)
    rot = salloc("rotS", [CH, CH], BF16)
    id16 = salloc("id16S", [CH, CH], BF16)
    id32 = salloc("id32S", [CH, CH], F32)
    id8 = salloc("id8S", [CH, CH], FP8)
    maskneg = salloc("masknegS", [CH, CH], F32)
    negones1 = salloc("negones1S", [1, CH], F32)
    cosfm = salloc("cosfmS", [D, TOK], BF16)
    sin2fm = salloc("sin2fmS", [D, TOK], BF16)
    gnormc = salloc("gnormcS", [D, 1], F32)
    dtbc = salloc("dtbcS", [32, 1], F32)
    alog = salloc("alogS", [32, 1], F32)
    nega = salloc("negaS", [32, 1], F32)
    ndtb = salloc("ndtbS", [32, 1], F32)
    fbias = salloc("fbiasS", [D, 32], F32)
    mskd = salloc("mskdS", [CH, 4], F32)
    ommd = salloc("ommdS", [CH, 4], F32)

    dttm = salloc("dttm", [CH, NCH * 32], F32)
    gtm = salloc("gtm", [CH, NCH * 32], F32)
    Gcol = salloc("Gcol", [CH, NCH * 32], F32)
    ecol = salloc("ecol", [CH, NCH * 32], F32)
    cdec = salloc("cdec", [CH, NCH * 32], F32)
    abc = salloc("abc", [CH, 32], F32)
    stats = salloc("stats", [CH, 64], F32)
    m2c = salloc("m2c", [CH, 8], F32)
    r2c = salloc("r2c", [CH, 8], F32)
    sc2 = salloc("sc2", [CH, 8], F32)
    nb2 = salloc("nb2", [CH, 8], F32)
    sclbc = salloc("sclbc", [CH, 4], F32)
    amaxsb = salloc("amaxsb", [1, 4], F32)
    sclrow = salloc("sclrow", [1, 4], F32)
    grow = salloc("grow", [1, CH], F32)
    small = salloc("small", [CH, 64], F32)
    x1f = salloc("x1f", [CH, NF * TOK], BF16)
    S = salloc("S", [CH, H * D], F32)
    ropeA = salloc("ropeA", [CH, TOK], BF16)
    ropeB = salloc("ropeB", [CH, TOK], BF16)
    kfm = [salloc(f"kfm{i}", [CH, TOK], BF16) for i in range(2)]
    ktm = [salloc(f"ktm{i}", [CH, TOK], BF16) for i in range(2)]
    vtm = [salloc(f"vtm{i}", [CH, TOK], BF16) for i in range(2)]
    xca = [salloc(f"xca{i}", [CH, TOK], BF16) for i in range(2)]
    kda = [salloc(f"kda{i}", [CH, TOK], BF16) for i in range(2)]
    ktmp = salloc("ktmp", [CH, TOK], BF16)
    dthm = salloc("dthm", [32, TOK], F32)
    ghm = salloc("ghm", [32, TOK], F32)
    winbf = salloc("winbf", [CH, NF * 32], BF16)
    wkbf = [salloc(f"wkbf{i}", [CH, HID], BF16) for i in range(2)]
    wvbf = [salloc(f"wvbf{i}", [CH, HID], BF16) for i in range(2)]

    pcm = {}
    def palloc(name, shape):
        cm = nc.psum_tensor(name, shape, F32)
        t = cm.__enter__()
        pcm[name] = cm
        return t
    pq = palloc("pq", [CH, 1024])
    pg = palloc("pg", [CH, 1024])
    pr = palloc("pr", [CH, 1024])
    ps1 = palloc("ps1", [CH, 512])
    ps2 = palloc("ps2", [CH, 512])

    sems = {}
    for s in ["sIn", "sW", "sScr", "sT", "sV", "sA", "sG", "sOut"]:
        sems[s] = Sem(nc.semaphore(s).__enter__())
    sIn, sW, sScr, sT, sV, sA, sG, sOut = (sems[k] for k in
        ["sIn", "sW", "sScr", "sT", "sV", "sA", "sG", "sOut"])

    block_cm = nc.Block()
    block = block_cm.__enter__()

    def colbcast(t, h):
        return t[:, :].rearrange("p (n c) -> p n c", c=32)[:, :, h:h + 1].broadcast_to(
            [CH, NCH, CH])

    def sec(name, fn):
        getattr(block, name)(fn)

    xt = salloc("xt", [CH, HID], F32)
    x1tm = salloc("x1tm", [CH, HID], BF16)
    fwt_t = salloc("fwt_t", [CH, 256], F32)

    def c_loads(sync):
        for nm, dst in [("tri", tri), ("ones128", ones128), ("ones1", ones1),
                        ("rot", rot), ("id16", id16), ("id32", id32), ("id8", id8),
                        ("maskneg", maskneg), ("cosfm", cosfm), ("sin2fm", sin2fm),
                        ("gnormc", gnormc), ("dtb", dtbc), ("alog", alog),
                        ("negones1", negones1),
                        ("fbias", fbias), ("mskd", mskd), ("ommd", ommd)]:
            sIn.inc(sync.dma_start(out=dst[:, :], in_=dram[nm]), 16)
    sec("sync", c_loads)
    CONSTS_IN = sIn.n

    def c_nega(scalar):
        scalar.wait_ge(sIn.h, CONSTS_IN)
        sA.inc(scalar.activation(nega[:, :], alog[:, :], ACTF.Exp))
    sec("scalar", c_nega)
    NEGA_A = sA.n

    def c_nega2(vector):
        vector.wait_ge(sA.h, NEGA_A)
        sV.inc(vector.tensor_scalar_mul(ndtb[:, :], dtbc[:, :], -1.0))
    sec("vector", c_nega2)

    x_loaded = []

    def p0_xload(sync):
        x_loaded.append(sIn.inc(sync.dma_start(
            out=xt[:, :], in_=dram["x"][0:CH, :]), 16))
    sec("sync", p0_xload)

    p0_x1_a = []
    p0_tr_t = []
    p0_rstd_v = []

    for tc in range(NCH):
        buf = xt
        ld = x_loaded[tc]
        so = 24 * (tc % 2)

        def p0_stats(vector, buf=buf, ld=ld, so=so):
            vector.wait_ge(sIn.h, ld)
            sV.inc(vector.reduce_sum(stats[:, so + 0:so + 1], buf[:, :], axis=AX.X))
        sec("vector", p0_stats)

        def p0_sq(scalar, buf=buf, ld=ld, so=so, tc=tc):
            scalar.wait_ge(sIn.h, ld)
            if tc >= 2:
                scalar.wait_ge(sV.h, p0_rstd_v[tc - 2])
            for q4 in range(4):
                sA.inc(scalar.activation(pq[:, :], buf[:, q4 * TOK:(q4 + 1) * TOK],
                                         ACTF.Square,
                                         accum_out=stats[:, so + 12 + q4:so + 13 + q4]))
        sec("scalar", p0_sq)
        sq_a = sA.n

        def p0_chain(vector, sq_a=sq_a, so=so):
            v = vector
            v.wait_ge(sA.h, sq_a)
            sV.inc(v.tensor_add(stats[:, so + 1:so + 2], stats[:, so + 12:so + 13], stats[:, so + 13:so + 14]))
            v.drain()
            sV.inc(v.tensor_add(stats[:, so + 16:so + 17], stats[:, so + 14:so + 15], stats[:, so + 15:so + 16]))
            v.drain()
            sV.inc(v.tensor_add(stats[:, so + 1:so + 2], stats[:, so + 1:so + 2], stats[:, so + 16:so + 17]))
            v.drain()
            sV.inc(v.tensor_scalar_mul(stats[:, so + 2:so + 3], stats[:, so + 0:so + 1], 1.0 / HID))
            v.drain()
            sV.inc(v.tensor_scalar_mul(stats[:, so + 3:so + 4], stats[:, so + 1:so + 2], 1.0 / HID))
            v.drain()
            sV.inc(v.tensor_mul(stats[:, so + 4:so + 5], stats[:, so + 2:so + 3], stats[:, so + 2:so + 3]))
            v.drain()
            sV.inc(v.tensor_sub(stats[:, so + 5:so + 6], stats[:, so + 3:so + 4], stats[:, so + 4:so + 5]))
            v.drain()
            sV.inc(v.tensor_scalar(stats[:, so + 5:so + 6], stats[:, so + 5:so + 6], 1.0, 1e-5,
                                   ALU.mult, ALU.add))
            v.drain()
        sec("vector", p0_chain)
        ch_v = sV.n

        def p0_sqrt(scalar, ch_v=ch_v, so=so):
            scalar.wait_ge(sV.h, ch_v)
            sA.inc(scalar.activation(stats[:, so + 6:so + 7], stats[:, so + 5:so + 6], ACTF.Sqrt))
        sec("scalar", p0_sqrt)
        sq2_a = sA.n

        def p0_rstd(vector, sq2_a=sq2_a, so=so):
            v = vector
            v.wait_ge(sA.h, sq2_a)
            sV.inc(v.reciprocal(stats[:, so + 7:so + 8], stats[:, so + 6:so + 7]))
            v.drain()
            sV.inc(v.tensor_mul(stats[:, so + 8:so + 9], stats[:, so + 7:so + 8], stats[:, so + 7:so + 8]))
            v.drain()
            sV.inc(v.tensor_mul(stats[:, so + 8:so + 9], stats[:, so + 8:so + 9], stats[:, so + 5:so + 6]))
            v.drain()
            sV.inc(v.tensor_scalar(stats[:, so + 8:so + 9], stats[:, so + 8:so + 9], -0.5, 1.5,
                                   ALU.mult, ALU.add))
            v.drain()
            sV.inc(v.tensor_mul(stats[:, so + 9:so + 10], stats[:, so + 7:so + 8], stats[:, so + 8:so + 9]))
            v.drain()
            sV.inc(v.tensor_mul(stats[:, so + 10:so + 11], stats[:, so + 2:so + 3], stats[:, so + 9:so + 10]))
            v.drain()
            sV.inc(v.tensor_scalar_mul(stats[:, so + 11:so + 12], stats[:, so + 10:so + 11], -1.0))
            v.drain()
        sec("vector", p0_rstd)
        p0_rstd_v.append(sV.n)
        rstd_v = sV.n

        def p0_x1(scalar, buf=buf, rstd_v=rstd_v, so=so, tc=tc):
            scalar.wait_ge(sV.h, rstd_v)
            if tc > 0:
                scalar.wait_ge(sT.h, p0_tr_t[tc - 1])
            sA.inc(scalar.activation(x1tm[:, :], buf[:, :], ACTF.Identity,
                                     bias=stats[:, so + 11:so + 12],
                                     scale=stats[:, so + 9:so + 10]))
        sec("scalar", p0_x1)
        x1_a = sA.n
        p0_x1_a.append(x1_a)

        for g4 in range(8):
            pst = ps1 if g4 % 2 == 0 else ps2

            def p0_tr(tensor, g4=g4, pst=pst, x1_a=x1_a):
                tensor.wait_ge(sA.h, x1_a)
                tensor.wait_ge(sV.h, sV.n)
                for i in range(4):
                    fc = g4 * 4 + i
                    sT.inc(tensor.transpose(pst[:, i * 64:(i + 1) * 64].bitcast(BF16),
                                            x1tm[:, fc * CH:(fc + 1) * CH],
                                            id16[:, :]))
            sec("tensor", p0_tr)
            tr_t = sT.n

            def p0_evac(vector, g4=g4, pst=pst, tr_t=tr_t, tc=tc):
                vector.wait_ge(sT.h, tr_t)
                dst = x1f[:, :].rearrange("p (fc t) -> p fc t", t=TOK)[
                    :, g4 * 4:(g4 + 1) * 4, tc * CH:(tc + 1) * CH]
                src = pst[:, 0:256].bitcast(BF16).rearrange("p (fc t) -> p fc t", t=CH)
                sV.inc(vector.tensor_copy(dst, src))
            sec("vector", p0_evac)
        p0_tr_t.append(sT.n)

        if tc + 1 < NCH:
            def p0_next(sync, tc=tc):
                sync.wait_ge(sA.h, p0_x1_a[tc])
                x_loaded.append(sIn.inc(sync.dma_start(
                    out=xt[:, :],
                    in_=dram["x"][(tc + 1) * CH:(tc + 2) * CH, :]), 16))
            sec("sync", p0_next)

    def p0_wam(sync):
        sScr.inc(sync.dma_start(out=fwt_t[:, :], in_=dram["fwsl"][0:CH, 0:256]), 16)
    sec("sync", p0_wam)

    for i in range(64):
        ld = sScr.n

        def p0_wred(vector, i=i, ld=ld):
            v = vector
            v.wait_ge(sScr.h, ld)
            sV.inc(v.reduce_max(stats[:, 48:49], fwt_t[:, :],
                                axis=AX.X, apply_absolute_value=True))
            v.drain()
            if i == 0:
                sV.inc(v.tensor_copy(stats[:, 54:55], stats[:, 48:49]))
            else:
                sV.inc(v.tensor_max(stats[:, 54:55], stats[:, 54:55], stats[:, 48:49]))
            v.drain()
        sec("vector", p0_wred)
        red_v = sV.n

        if i + 1 < 64:
            def p0_wnext(sync, i=i, red_v=red_v):
                sync.wait_ge(sV.h, red_v)
                r, cq = divmod(i + 1, 16)
                sScr.inc(sync.dma_start(out=fwt_t[:, :],
                                        in_=dram["fwsl"][r * CH:(r + 1) * CH,
                                                         cq * 256:(cq + 1) * 256]), 16)
            sec("sync", p0_wnext)
    wcol_v = sV.n

    def p0_wtr(tensor):
        tensor.wait_ge(sV.h, wcol_v)
        tensor.wait_ge(sV.h, sV.n)
        sT.inc(tensor.transpose(ps1[0:1, 0:CH], stats[:, 54:55], id32[:, :]))
    sec("tensor", p0_wtr)
    wtr_t = sT.n

    def p0_wfin(vector, wtr_t=wtr_t):
        vector.wait_ge(sT.h, wtr_t)
        sV.inc(vector.reduce_max(amaxsb[0:1, 1:2], ps1[0:1, 0:CH], axis=AX.X))
    sec("vector", p0_wfin)

    P0_T, P0_V, P0_A = sT.n, sV.n, sA.n
    sfree("fwt_t", "x1tm", "xt")


    def p1_win(gpsimd):
        gpsimd.wait_ge(sA.h, P0_A)
        gpsimd.wait_ge(sV.h, P0_V)
        gpsimd.wait_ge(sT.h, P0_T)
        src = dram["winT"].rearrange("(fc f) o -> f fc o", f=CH)
        dst = winbf[:, :].rearrange("f (fc o) -> f fc o", o=32)
        sW.inc(gpsimd.dma_start(out=dst, in_=src), 16)
    sec("gpsimd", p1_win)
    win_w = sW.n
    X1_ALL = P0_V

    def p1_dtmm(tensor):
        tensor.wait_ge(sW.h, win_w)
        tensor.wait_ge(sV.h, X1_ALL)
        for fc in range(NF):
            for th in range(2):
                sT.inc(tensor.matmul(
                    pq[0:32, th * 512:(th + 1) * 512],
                    winbf[:, fc * 32:(fc + 1) * 32],
                    x1f[:, fc * TOK + th * 512: fc * TOK + (th + 1) * 512],
                    start=(fc == 0), stop=(fc == NF - 1)))
    sec("tensor", p1_dtmm)
    dt_t = sT.n

    def p1_dthm(scalar):
        s = scalar
        s.wait_ge(sT.h, dt_t)
        s.wait_ge(sV.h, sV.n)
        sA.inc(s.activation(ghm[:, :], pq[0:32, :], ACTF.Sigmoid,
                            bias=ndtb[:, :], scale=-1.0))
        sA.inc(s.activation(dthm[:, :], ghm[:, :], ACTF.Ln))
    sec("scalar", p1_dthm)
    dthm_a = sA.n

    def p1_ghm(vector):
        v = vector
        v.wait_ge(sA.h, dthm_a)
        sV.inc(v.tensor_scalar(ghm[:, :], dthm[:, :], nega[:, :], None, ALU.mult))
        sV.inc(v.tensor_scalar_mul(dthm[:, :], dthm[:, :], -1.0))
    sec("vector", p1_ghm)
    ghm_v = sV.n

    for n in range(NCH):
        def p1_tdg(tensor, n=n):
            tensor.wait_ge(sV.h, ghm_v)
            tensor.wait_ge(sV.h, sV.n)
            sT.inc(tensor.transpose(ps1[:, 0:32], dthm[:, n * CH:(n + 1) * CH], id32[0:32, 0:32]))
            sT.inc(tensor.transpose(ps1[:, 32:64], ghm[:, n * CH:(n + 1) * CH], id32[0:32, 0:32]))
        sec("tensor", p1_tdg)
        tdg_t = sT.n

        def p1_edg(vector, n=n, tdg_t=tdg_t):
            vector.wait_ge(sT.h, tdg_t)
            sV.inc(vector.tensor_copy(dttm[:, n * 32:(n + 1) * 32], ps1[:, 0:32]))
            sV.inc(vector.tensor_copy(gtm[:, n * 32:(n + 1) * 32], ps1[:, 32:64]))
        sec("vector", p1_edg)

    GTM_V = sV.n

    def p1_abc0(vector):
        sV.inc(vector.memset(abc[:, :], 1.0))
    sec("vector", p1_abc0)

    for n in range(NCH):
        def p1_gmm(tensor, n=n):
            tensor.wait_ge(sV.h, GTM_V)
            tensor.wait_ge(sV.h, sV.n)
            tensor.wait_ge(sA.h, sA.n)
            sT.inc(tensor.matmul(ps2[:, 0:32], tri[:, :],
                                 gtm[:, n * 32:(n + 1) * 32], start=True, stop=True))
            sT.inc(tensor.matmul(ps2[:, 32:64], ones128[:, :],
                                 gtm[:, n * 32:(n + 1) * 32], start=True, stop=True))
        sec("tensor", p1_gmm)
        gmm_t = sT.n

        def p1_gev(vector, n=n, gmm_t=gmm_t):
            v = vector
            v.wait_ge(sT.h, gmm_t)
            sV.inc(v.tensor_copy(Gcol[:, n * 32:(n + 1) * 32], ps2[:, 0:32]))
            v.drain()
            sV.inc(v.tensor_sub(small[:, 0:32], ps2[:, 32:64], Gcol[:, n * 32:(n + 1) * 32]))
            v.drain()
        sec("vector", p1_gev)
        gev_v = sV.n

        def p1_gex(scalar, n=n, gev_v=gev_v):
            s = scalar
            s.wait_ge(sV.h, gev_v)
            sA.inc(s.activation(ecol[:, n * 32:(n + 1) * 32], small[:, 0:32], ACTF.Exp))
            sA.inc(s.activation(cdec[:, n * 32:(n + 1) * 32], ps2[:, 32:64], ACTF.Exp))
        sec("scalar", p1_gex)
        gex_a = sA.n

        def p1_abc(vector, n=n, gex_a=gex_a):
            vector.wait_ge(sA.h, gex_a)
            vector.drain()
            sV.inc(vector.tensor_mul(abc[:, :], abc[:, :], cdec[:, n * 32:(n + 1) * 32]))
        sec("vector", p1_abc)

    def p1_szero(vector):
        sV.inc(vector.memset(S[:, :], 0.0))
    sec("vector", p1_szero)

    kv_w_loaded = []

    def p1_kvw0(gpsimd):
        for kv in range(2):
            for nm, dst in [("wkb", wkbf[kv]), ("wvb", wvbf[kv])]:
                src = dram[nm][kv * HID:(kv + 1) * HID, :].rearrange(
                    "(fc f) o -> f fc o", f=CH)
                d = dst[:, :].rearrange("f (fc o) -> f fc o", o=CH)
                sW.inc(gpsimd.dma_start(out=d, in_=src), 16)
            kv_w_loaded.append(sW.n)
    sec("gpsimd", p1_kvw0)

    kvout_scr = []

    for kv in range(KVH):
        bi = kv % 2
        wtar = kv_w_loaded[kv]

        def p1_kmm(tensor, bi=bi, wtar=wtar):
            tensor.wait_ge(sW.h, wtar)
            tensor.wait_ge(sV.h, sV.n)
            tensor.wait_ge(sA.h, sA.n)
            for fc in range(NF):
                for th in range(2):
                    sT.inc(tensor.matmul(
                        pq[:, th * 512:(th + 1) * 512],
                        wkbf[bi][:, fc * CH:(fc + 1) * CH],
                        x1f[:, fc * TOK + th * 512: fc * TOK + (th + 1) * 512],
                        start=(fc == 0), stop=(fc == NF - 1)))
        sec("tensor", p1_kmm)
        kmm_t = sT.n

        def p1_vmm(tensor, bi=bi, wtar=wtar):
            tensor.wait_ge(sW.h, wtar)
            for fc in range(NF):
                for th in range(2):
                    sT.inc(tensor.matmul(
                        pg[:, th * 512:(th + 1) * 512],
                        wvbf[bi][:, fc * CH:(fc + 1) * CH],
                        x1f[:, fc * TOK + th * 512: fc * TOK + (th + 1) * 512],
                        start=(fc == 0), stop=(fc == NF - 1)))
        sec("tensor", p1_vmm)
        vmm_t = sT.n

        def p1_kev(vector, kmm_t=kmm_t):
            vector.wait_ge(sT.h, kmm_t)
            sV.inc(vector.tensor_copy(ktmp[:, :], pq[:, :]))
        sec("vector", p1_kev)
        kev_v = sV.n

        def p1_krot(tensor, kev_v=kev_v):
            tensor.wait_ge(sV.h, kev_v)
            for th in range(2):
                sT.inc(tensor.matmul(pr[:, th * 512:(th + 1) * 512], rot[:, :],
                                     ktmp[:, th * 512:(th + 1) * 512],
                                     start=True, stop=True))
        sec("tensor", p1_krot)
        krot_t = sT.n

        def p1_krope(vector, bi=bi, kv=kv, krot_t=krot_t):
            v = vector
            v.wait_ge(sT.h, krot_t)
            if kv >= 2:
                v.wait_ge(sScr.h, kvout_scr[kv - 2])
            sV.inc(v.tensor_mul(ropeA[:, :], ktmp[:, :], cosfm[:, :]))
            sV.inc(v.tensor_mul(ropeB[:, :], pr[:, :], sin2fm[:, :]))
            sV.inc(v.tensor_add(kfm[bi][:, :], ropeA[:, :], ropeB[:, :]))
        sec("vector", p1_krope)
        krope_v = sV.n

        def p1_ktr(tensor, bi=bi, krope_v=krope_v):
            tensor.wait_ge(sV.h, krope_v)
            tensor.wait_ge(sA.h, sA.n)
            for n in range(NCH):
                sT.inc(tensor.transpose(ps1[:, n * 64:(n + 1) * 64].bitcast(BF16),
                                        kfm[bi][:, n * CH:(n + 1) * CH], id16[:, :]))
        sec("tensor", p1_ktr)
        ktr_t = sT.n

        def p1_kte(scalar, bi=bi, kv=kv, ktr_t=ktr_t):
            scalar.wait_ge(sT.h, ktr_t)
            if kv >= 2:
                scalar.wait_ge(sScr.h, kvout_scr[kv - 2])
            sA.inc(scalar.copy(ktm[bi][:, :], ps1[:, 0:512].bitcast(BF16)))
        sec("scalar", p1_kte)

        def p1_vev(vector, vmm_t=vmm_t):
            vector.wait_ge(sT.h, vmm_t)
            sV.inc(vector.tensor_copy(ktmp[:, :], pg[:, :]))
        sec("vector", p1_vev)
        vev_v = sV.n

        def p1_vtr(tensor, vev_v=vev_v):
            tensor.wait_ge(sV.h, vev_v)
            tensor.wait_ge(sA.h, sA.n)
            for n in range(NCH):
                sT.inc(tensor.transpose(ps1[:, n * 64:(n + 1) * 64].bitcast(BF16),
                                        ktmp[:, n * CH:(n + 1) * CH], id16[:, :]))
        sec("tensor", p1_vtr)
        vtr_t = sT.n

        def p1_vte(scalar, bi=bi, kv=kv, vtr_t=vtr_t):
            scalar.wait_ge(sT.h, vtr_t)
            if kv >= 2:
                scalar.wait_ge(sScr.h, kvout_scr[kv - 2])
            sA.inc(scalar.copy(vtm[bi][:, :], ps1[:, 0:512].bitcast(BF16)))
        sec("scalar", p1_vte)

        kv_ready_a = sA.n

        def p1_kvout(sync, kv=kv, bi=bi, kv_ready_a=kv_ready_a, krope_v=krope_v):
            sync.wait_ge(sA.h, kv_ready_a)
            sync.wait_ge(sV.h, krope_v)
            base = kv * 3 * CH
            sScr.inc(sync.dma_start(out=kvscr[base:base + CH, :], in_=kfm[bi][:, :]), 16)
            sScr.inc(sync.dma_start(out=kvscr[base + CH:base + 2 * CH, :], in_=ktm[bi][:, :]), 16)
            sScr.inc(sync.dma_start(out=kvscr[base + 2 * CH:base + 3 * CH, :], in_=vtm[bi][:, :]), 16)
        sec("sync", p1_kvout)
        kvout_scr.append(sScr.n)

        if kv + 2 < KVH:
            def p1_kvw(gpsimd, kv=kv, bi=bi, vmm_t=vmm_t):
                gpsimd.wait_ge(sT.h, vmm_t)
                for nm, dst in [("wkb", wkbf[bi]), ("wvb", wvbf[bi])]:
                    src = dram[nm][(kv + 2) * HID:(kv + 3) * HID, :].rearrange(
                        "(fc f) o -> f fc o", f=CH)
                    d = dst[:, :].rearrange("f (fc o) -> f fc o", o=CH)
                    sW.inc(gpsimd.dma_start(out=d, in_=src), 16)
                kv_w_loaded.append(sW.n)
            sec("gpsimd", p1_kvw)

        for h in range(4 * kv, 4 * kv + 4):
            hb = h % 2

            def p1_xka(vector, bi=bi, h=h, hb=hb, kv_ready_a=kv_ready_a):
                v = vector
                v.wait_ge(sA.h, kv_ready_a)
                v.wait_ge(sT.h, sT.n)
                i3 = vtm[bi][:, :].rearrange("p (n d) -> p n d", d=CH)
                o3 = xca[hb][:, :].rearrange("p (n d) -> p n d", d=CH)
                sV.inc(v.tensor_tensor(o3, i3, colbcast(dttm, h), ALU.mult))
                i3 = ktm[bi][:, :].rearrange("p (n d) -> p n d", d=CH)
                o3 = kda[hb][:, :].rearrange("p (n d) -> p n d", d=CH)
                sV.inc(v.tensor_tensor(o3, i3, colbcast(ecol, h), ALU.mult))
            sec("vector", p1_xka)
            xk_v = sV.n

            for n in range(NCH):
                col = n * 32 + h

                def p1_cst(tensor, hb=hb, n=n, xk_v=xk_v):
                    tensor.wait_ge(sV.h, sV.n if n > 0 else xk_v)
                    sT.inc(tensor.matmul(ps2[:, 0:CH],
                                         kda[hb][:, n * CH:(n + 1) * CH],
                                         xca[hb][:, n * CH:(n + 1) * CH],
                                         start=True, stop=True))
                sec("tensor", p1_cst)
                cst_t = sT.n

                def p1_bupd(vector, h=h, col=col, cst_t=cst_t):
                    vector.wait_ge(sT.h, cst_t)
                    sV.inc(vector.scalar_tensor_tensor(
                        S[:, h * D:(h + 1) * D], S[:, h * D:(h + 1) * D],
                        cdec[:, col:col + 1], ps2[:, 0:CH], ALU.mult, ALU.add))
                sec("vector", p1_bupd)

    B_done_v = sV.n

    def p1_agout(sync):
        sync.wait_ge(sV.h, B_done_v)
        sScr.inc(sync.dma_start(out=ag_in[:, 0:4096], in_=S[:, :]), 16)
        sScr.inc(sync.dma_start(out=ag_in[:, 4096:4128], in_=abc[:, :]), 16)
    sec("sync", p1_agout)
    ag_dma = sScr.n

    def p1_ag(gpsimd):
        gpsimd.wait_ge(sScr.h, ag_dma)
        sG.inc(gpsimd.collective_compute(
            "AllGather", ALU.bypass,
            replica_groups=[[0, 1, 2, 3], [4, 5, 6, 7]],
            ins=[ag_in], outs=[ag_out]))
    sec("gpsimd", p1_ag)
    ag_g = sG.n

    agtmp = salloc("agtmp", [CH, 4128], F32)
    alpha = salloc("alpha", [CH, 32], F32)

    def p1_szero2(vector):
        vector.wait_ge(sScr.h, ag_dma)
        sV.inc(vector.memset(S[:, :], 0.0))
    sec("vector", p1_szero2)

    for d_ in range(4):
        def p1_agin(sync, d_=d_, ag_g=ag_g):
            sync.wait_ge(sG.h, ag_g)
            sync.wait_ge(sV.h, sV.n)
            sScr.inc(sync.dma_start(out=agtmp[:, :],
                                    in_=ag_out[d_ * CH:(d_ + 1) * CH, :]), 16)
        sec("sync", p1_agin)
        agin_d = sScr.n

        def p1_sin(vector, d_=d_, agin_d=agin_d):
            v = vector
            v.wait_ge(sScr.h, agin_d)
            sV.inc(v.tensor_scalar(agtmp[:, 0:4096], agtmp[:, 0:4096],
                                   mskd[:, d_:d_ + 1], None, ALU.mult))
            sV.inc(v.tensor_scalar(alpha[:, :], agtmp[:, 4096:4128],
                                   mskd[:, d_:d_ + 1], ommd[:, d_:d_ + 1],
                                   ALU.mult, ALU.add))
            v.drain()
            for h in range(H):
                sV.inc(v.scalar_tensor_tensor(
                    S[:, h * D:(h + 1) * D], S[:, h * D:(h + 1) * D],
                    alpha[:, h:h + 1], agtmp[:, h * D:(h + 1) * D],
                    ALU.mult, ALU.add))
        sec("vector", p1_sin)

    SIN_V = sV.n
    P1_T = sT.n
    sfree("alpha", "agtmp", "wvbf1", "wvbf0", "wkbf1", "wkbf0", "winbf",
          "ghm", "dthm", "ktmp")

    wqbf = [salloc(f"wqbf{i}", [CH, HID], BF16) for i in range(2)]
    wgbf = [salloc(f"wgbf{i}", [CH, HID], BF16) for i in range(2)]
    qtmp = salloc("qtmp", [CH, TOK], BF16)
    qr_t = [salloc(f"qr_t{i}", [CH, TOK], BF16) for i in range(2)]
    gs_t = [salloc(f"gs_t{i}", [CH, TOK], BF16) for i in range(2)]
    sbf_t = salloc("sbf_t", [CH, CH], BF16)
    growa = salloc("growa", [1, TOK], F32)
    deca = [salloc(f"deca{i}", [CH, TOK], F32) for i in range(2)]
    ega = salloc("ega", [CH, TOK], BF16)
    qga = [salloc(f"qga{i}", [CH, TOK], BF16) for i in range(2)]
    sd_t = salloc("sd_t", [CH, CH], BF16)
    ygh = [salloc(f"ygh{i}", [CH, TOK], BF16) for i in range(2)]
    ybuf = [salloc(f"ybuf{i}", [CH, TOK], BF16) for i in range(2)]
    ssqa = salloc("ssqa", [CH, 48], F32)
    sq128 = salloc("sq128", [CH, CH], BF16)

    q_w_loaded = []

    def p2_w0(gpsimd):
        for hh in range(2):
            for nm, dst in [("wqb", wqbf[hh]), ("wgb", wgbf[hh])]:
                src = dram[nm][hh * HID:(hh + 1) * HID, :].rearrange(
                    "(fc f) o -> f fc o", f=CH)
                d = dst[:, :].rearrange("f (fc o) -> f fc o", o=CH)
                sW.inc(gpsimd.dma_start(out=d, in_=src), 16)
            q_w_loaded.append(sW.n)
    sec("gpsimd", p2_w0)

    kvbuf_loaded = []

    def p2_kv0(sync):
        sync.wait_ge(sScr.h, ag_dma)
        for kvi in range(2):
            base = kvi * 3 * CH
            sScr.inc(sync.dma_start(out=kfm[kvi][:, :], in_=kvscr[base:base + CH, :]), 16)
            sScr.inc(sync.dma_start(out=ktm[kvi][:, :], in_=kvscr[base + CH:base + 2 * CH, :]), 16)
            sScr.inc(sync.dma_start(out=vtm[kvi][:, :], in_=kvscr[base + 2 * CH:base + 3 * CH, :]), 16)
            kvbuf_loaded.append(sScr.n)
    sec("sync", p2_kv0)

    ygout_scr = []
    silu_prev = [0]
    ega_xka_prev = [0]
    egact_prev = [0]
    ygh_prev = {}

    def emit_prep(h):
        kv = h // 4
        bi = h % 2
        kvbi = kv % 2
        wtar = q_w_loaded[h]
        kvtar = kvbuf_loaded[kv]
        st = {}

        def p2_qmm(tensor, bi=bi, wtar=wtar, sp=silu_prev[0]):
            t = tensor
            t.wait_ge(sW.h, wtar)
            if sp:
                t.wait_ge(sA.h, sp)
            if h == 0:
                t.wait_ge(sV.h, SIN_V)
            for fc in range(NF):
                for th in range(2):
                    sT.inc(t.matmul(
                        pq[:, th * 512:(th + 1) * 512],
                        wqbf[bi][:, fc * CH:(fc + 1) * CH],
                        x1f[:, fc * TOK + th * 512: fc * TOK + (th + 1) * 512],
                        start=(fc == 0), stop=(fc == NF - 1)))
        sec("tensor", p2_qmm)
        qmm_t = sT.n

        def p2_qev(vector, qmm_t=qmm_t):
            vector.wait_ge(sT.h, qmm_t)
            vector.wait_ge(sT.h, sT.n if False else qmm_t)
            sV.inc(vector.tensor_copy(qtmp[:, :], pq[:, :]))
        sec("vector", p2_qev)
        qev_v = sV.n

        def p2_gmm(tensor, bi=bi, qev_v=qev_v):
            t = tensor
            t.wait_ge(sV.h, qev_v)
            for fc in range(NF):
                for th in range(2):
                    sT.inc(t.matmul(
                        pq[:, th * 512:(th + 1) * 512],
                        wgbf[bi][:, fc * CH:(fc + 1) * CH],
                        x1f[:, fc * TOK + th * 512: fc * TOK + (th + 1) * 512],
                        start=(fc == 0), stop=(fc == NF - 1)))
        sec("tensor", p2_gmm)
        gmm_t = sT.n

        def p2_gs(scalar, gmm_t=gmm_t, bi=bi, h=h):
            s = scalar
            s.wait_ge(sT.h, gmm_t)
            if h >= 2:
                s.wait_ge(sV.h, ygh_prev[h - 2])
            sA.inc(s.activation(gs_t[bi][:, :], pq[:, :], ACTF.Silu))
        sec("scalar", p2_gs)
        st["silu_a"] = sA.n
        silu_prev[0] = sA.n

        def p2_qrot(tensor, qev_v=qev_v, ea=egact_prev[0]):
            t = tensor
            t.wait_ge(sV.h, qev_v)
            if ea:
                t.wait_ge(sA.h, ea)
            for th in range(2):
                sT.inc(t.matmul(pr[:, th * 512:(th + 1) * 512], rot[:, :],
                                qtmp[:, th * 512:(th + 1) * 512],
                                start=True, stop=True))
        sec("tensor", p2_qrot)
        qrot_t = sT.n

        def p2_qrope(vector, qrot_t=qrot_t, bi=bi):
            v = vector
            v.wait_ge(sT.h, qrot_t)
            v.wait_ge(sT.h, sT.n)
            sV.inc(v.tensor_mul(ropeA[:, :], qtmp[:, :], cosfm[:, :]))
            sV.inc(v.tensor_mul(ropeB[:, :], pr[:, :], sin2fm[:, :]))
            sV.inc(v.tensor_add(qr_t[bi][:, :], ropeA[:, :], ropeB[:, :]))
        sec("vector", p2_qrope)
        qrope_v = sV.n
        st["qrope_v"] = qrope_v

        if h + 2 < H:
            def p2_wn(gpsimd, h=h, bi=bi, gmm_t=gmm_t):
                gpsimd.wait_ge(sT.h, gmm_t)
                for nm, dst in [("wqb", wqbf[bi]), ("wgb", wgbf[bi])]:
                    src = dram[nm][(h + 2) * HID:(h + 3) * HID, :].rearrange(
                        "(fc f) o -> f fc o", f=CH)
                    d = dst[:, :].rearrange("f (fc o) -> f fc o", o=CH)
                    sW.inc(gpsimd.dma_start(out=d, in_=src), 16)
                q_w_loaded.append(sW.n)
            sec("gpsimd", p2_wn)

        def p2_grows(tensor, h=h, qrope_v=qrope_v):
            t = tensor
            t.wait_ge(sV.h, qrope_v)
            for n in range(NCH):
                c0 = (n % 4) * CH
                ps = pr[0:1, 0:512] if n < 4 else pr[0:1, 512:1024]
                sT.inc(t.matmul(ps[0:1, c0:c0 + CH], gtm[:, n * 32 + h:n * 32 + h + 1],
                                tri[:, :], start=True, stop=True))
        sec("tensor", p2_grows)
        grows_t = sT.n

        def p2_growev(scalar, grows_t=grows_t):
            s = scalar
            s.wait_ge(sT.h, grows_t)
            sA.inc(s.copy(growa[0:1, :], pr[0:1, :]))
        sec("scalar", p2_growev)
        growev_a = sA.n

        def p2_m1mm(tensor, growev_a=growev_a):
            t = tensor
            t.wait_ge(sA.h, growev_a)
            for n in range(NCH):
                sT.inc(t.matmul(pr[:, n * CH:(n + 1) * CH], ones1[0:1, :],
                                growa[0:1, n * CH:(n + 1) * CH], start=True, stop=False))
                sT.inc(t.matmul(pr[:, n * CH:(n + 1) * CH],
                                growa[0:1, n * CH:(n + 1) * CH], negones1[0:1, :],
                                start=False, stop=True))
        sec("tensor", p2_m1mm)
        m1mm_t = sT.n

        def p2_m1add(vector, m1mm_t=m1mm_t, bi=bi):
            vector.wait_ge(sT.h, m1mm_t)
            p3 = pr[:, :].rearrange("p (n i) -> p n i", i=CH)
            mk = maskneg[:, :].unsqueeze(1).broadcast_to([CH, NCH, CH])
            o3 = deca[bi][:, :].rearrange("p (n i) -> p n i", i=CH)
            sV.inc(vector.tensor_tensor(o3, p3, mk, ALU.add))
        sec("vector", p2_m1add)
        m1add_v = sV.n

        def p2_dec(scalar, m1add_v=m1add_v, bi=bi):
            scalar.wait_ge(sV.h, m1add_v)
            sA.inc(scalar.activation(deca[bi][:, :], deca[bi][:, :], ACTF.Exp))
        sec("scalar", p2_dec)
        st["dec_a"] = sA.n

        def p2_egmm(tensor, m1add_v=m1add_v):
            t = tensor
            t.wait_ge(sV.h, m1add_v)
            for n in range(NCH):
                sT.inc(t.matmul(pr[:, n * CH:(n + 1) * CH], ones1[0:1, :],
                                growa[0:1, n * CH:(n + 1) * CH], start=True, stop=True))
        sec("tensor", p2_egmm)
        egmm_t = sT.n

        def p2_eg(scalar, egmm_t=egmm_t, ea=ega_xka_prev[0]):
            s = scalar
            s.wait_ge(sT.h, egmm_t)
            if ea:
                s.wait_ge(sV.h, ea)
            sA.inc(s.activation(ega[:, :], pr[:, :], ACTF.Exp))
        sec("scalar", p2_eg)
        eg_a = sA.n
        egact_prev[0] = eg_a

        def p2_xka(vector, h=h, kvbi=kvbi, bi=bi, kvtar=kvtar, eg_a=eg_a):
            v = vector
            v.wait_ge(sScr.h, kvtar)
            v.wait_ge(sT.h, sT.n)
            v.wait_ge(sA.h, eg_a)
            i3 = vtm[kvbi][:, :].rearrange("p (n d) -> p n d", d=CH)
            o3 = xca[bi][:, :].rearrange("p (n d) -> p n d", d=CH)
            sV.inc(v.tensor_tensor(o3, i3, colbcast(dttm, h), ALU.mult))
            i3 = ktm[kvbi][:, :].rearrange("p (n d) -> p n d", d=CH)
            o3 = kda[bi][:, :].rearrange("p (n d) -> p n d", d=CH)
            sV.inc(v.tensor_tensor(o3, i3, colbcast(ecol, h), ALU.mult))
            sV.inc(v.tensor_mul(qga[bi][:, :], qr_t[bi][:, :], ega[:, :]))
        sec("vector", p2_xka)
        st["xka_v"] = sV.n
        ega_xka_prev[0] = sV.n
        return st

    def emit_nloop(h, st):
        kv = h // 4
        bi = h % 2
        kvbi = kv % 2
        sd_vs = []
        supd_prev = [0]
        yev_prev = [0]
        sqa_prev = [0]
        for n in range(NCH):
            col = n * 32 + h

            def p2_sbf(vector, h=h):
                sV.inc(vector.tensor_copy(sbf_t[:, :], S[:, h * D:(h + 1) * D]))
            sec("vector", p2_sbf)
            sbf_v = sV.n

            sbank = pg[:, 0:CH] if n % 2 == 0 else ps2[:, 256:384]

            def p2_mm1(tensor, n=n, h=h, bi=bi, kvbi=kvbi, st=st, sbank=sbank):
                t = tensor
                if n == 0:
                    t.wait_ge(sV.h, st["xka_v"])
                else:
                    t.wait_ge(sV.h, supd_prev[0])
                if n == 1 and h >= 1:
                    t.wait_ge(sV.h, ygh_prev.get(h - 1, 0))
                if n >= 2:
                    t.wait_ge(sV.h, sd_vs[n - 2])
                sT.inc(t.matmul(pg[:, 512:640], kda[bi][:, n * CH:(n + 1) * CH],
                                xca[bi][:, n * CH:(n + 1) * CH], start=True, stop=True))
                sT.inc(t.matmul(sbank, kfm[kvbi][:, n * CH:(n + 1) * CH],
                                qr_t[bi][:, n * CH:(n + 1) * CH], start=True, stop=True))
            sec("tensor", p2_mm1)
            mm1_t = sT.n

            def p2_sd(vector, n=n, mm1_t=mm1_t, st=st, bi=bi, sbank=sbank):
                v = vector
                v.wait_ge(sT.h, mm1_t)
                if n == 0:
                    v.wait_ge(sA.h, st["dec_a"])
                sV.inc(v.tensor_mul(sd_t[:, :], sbank,
                                    deca[bi][:, n * CH:(n + 1) * CH]))
            sec("vector", p2_sd)
            sd_v = sV.n
            sd_vs.append(sd_v)

            def p2_ymm(tensor, n=n, bi=bi, sd_v=sd_v):
                t = tensor
                t.wait_ge(sV.h, sd_v)
                if yev_prev[0]:
                    t.wait_ge(sV.h, yev_prev[0])
                if sqa_prev[0]:
                    t.wait_ge(sA.h, sqa_prev[0])
                sT.inc(t.matmul(ps1[:, 0:CH], sd_t[:, :],
                                xca[bi][:, n * CH:(n + 1) * CH], start=True, stop=False))
                sT.inc(t.matmul(ps1[:, 0:CH], qga[bi][:, n * CH:(n + 1) * CH],
                                sbf_t[:, :], start=False, stop=True))
            sec("tensor", p2_ymm)
            ymm_t = sT.n

            def p2_supd(vector, h=h, col=col, mm1_t=mm1_t):
                vector.wait_ge(sT.h, mm1_t)
                sV.inc(vector.scalar_tensor_tensor(
                    S[:, h * D:(h + 1) * D], S[:, h * D:(h + 1) * D],
                    cdec[:, col:col + 1], pg[:, 512:640], ALU.mult, ALU.add))
            sec("vector", p2_supd)
            supd_prev[0] = sV.n

            def p2_sqa(scalar, n=n, bi=bi, ymm_t=ymm_t):
                s = scalar
                s.wait_ge(sT.h, ymm_t)
                sA.inc(s.activation(sq128[:, :], ps1[:, 0:CH], ACTF.Square,
                                    accum_out=ssqa[:, bi * 24 + n:bi * 24 + n + 1]))
            sec("scalar", p2_sqa)
            sqa_prev[0] = sA.n

            def p2_yev(vector, n=n, bi=bi, ymm_t=ymm_t):
                vector.wait_ge(sT.h, ymm_t)
                sV.inc(vector.tensor_copy(ybuf[bi][:, n * CH:(n + 1) * CH],
                                          ps1[:, 0:CH]))
            sec("vector", p2_yev)
            yev_prev[0] = sV.n

        st["loop_v"] = sV.n
        st["loop_a"] = sA.n

        if h % 4 == 3 and kv + 2 < KVH:
            def p2_kvn(sync, kv=kv, kvbi=kvbi):
                sync.wait_ge(sT.h, sT.n)
                sync.wait_ge(sA.h, sA.n)
                sync.wait_ge(sV.h, sV.n)
                base = (kv + 2) * 3 * CH
                sScr.inc(sync.dma_start(out=kfm[kvbi][:, :], in_=kvscr[base:base + CH, :]), 16)
                sScr.inc(sync.dma_start(out=ktm[kvbi][:, :], in_=kvscr[base + CH:base + 2 * CH, :]), 16)
                sScr.inc(sync.dma_start(out=vtm[kvbi][:, :], in_=kvscr[base + 2 * CH:base + 3 * CH, :]), 16)
                kvbuf_loaded.append(sScr.n)
            sec("sync", p2_kvn)

    def emit_tail(h, st):
        bi = h % 2

        def p2_rms8(vector, st=st, bi=bi):
            v = vector
            v.wait_ge(sA.h, st["loop_a"])
            v.wait_ge(sV.h, st["loop_v"])
            v.drain()
            sV.inc(v.tensor_scalar(ssqa[:, bi * 24 + 8:bi * 24 + 16],
                                   ssqa[:, bi * 24:bi * 24 + 8],
                                   1.0 / 128.0, 1e-5, ALU.mult, ALU.add))
            v.drain()
        sec("vector", p2_rms8)
        rms8_v = sV.n

        def p2_rms8a(scalar, rms8_v=rms8_v, bi=bi):
            s = scalar
            s.wait_ge(sV.h, rms8_v)
            sA.inc(s.activation(ssqa[:, bi * 24 + 16:bi * 24 + 24],
                                ssqa[:, bi * 24 + 8:bi * 24 + 16], ACTF.Ln))
            s.drain()
            sA.inc(s.activation(ssqa[:, bi * 24 + 8:bi * 24 + 16],
                                ssqa[:, bi * 24 + 16:bi * 24 + 24], ACTF.Exp, scale=-0.5))
        sec("scalar", p2_rms8a)
        rms8_a = sA.n

        def p2_yna(vector, bi=bi, rms8_a=rms8_a):
            v = vector
            v.wait_ge(sA.h, rms8_a)
            i3 = ybuf[bi][:, :].rearrange("p (n d) -> p n d", d=CH)
            rmsb = ssqa[:, bi * 24 + 8:bi * 24 + 16].unsqueeze(2).broadcast_to(
                [CH, NCH, CH])
            sV.inc(v.tensor_tensor(i3, i3, rmsb, ALU.mult))
        sec("vector", p2_yna)
        yna_v = sV.n

        yg_prev = [0]
        for n in range(NCH):
            def p2_ytr(tensor, n=n, bi=bi, yna_v=yna_v):
                t = tensor
                t.wait_ge(sV.h, yna_v)
                if yg_prev[0]:
                    t.wait_ge(sV.h, yg_prev[0])
                sT.inc(t.transpose(ps2[:, 0:64].bitcast(BF16),
                                   ybuf[bi][:, n * CH:(n + 1) * CH], id16[:, :]))
            sec("tensor", p2_ytr)
            ytr_t = sT.n

            def p2_yg(vector, h=h, n=n, bi=bi, ytr_t=ytr_t):
                v = vector
                v.wait_ge(sT.h, ytr_t)
                if n == 0 and h >= 2:
                    v.wait_ge(sScr.h, ygout_scr[h - 2])
                sV.inc(v.scalar_tensor_tensor(
                    ygh[bi][:, n * CH:(n + 1) * CH], ps2[:, 0:64].bitcast(BF16),
                    gnormc[:, 0:1],
                    gs_t[bi][:, n * CH:(n + 1) * CH], ALU.mult, ALU.mult))
            sec("vector", p2_yg)
            yg_prev[0] = sV.n

        ygh_prev[h] = sV.n

        def p2_ygout(sync, h=h, bi=bi, t=sV.n):
            sync.wait_ge(sV.h, t)
            sScr.inc(sync.dma_start(out=ygscr[h * CH:(h + 1) * CH, :], in_=ygh[bi][:, :]), 16)
        sec("sync", p2_ygout)
        ygout_scr.append(sScr.n)

    prep_st = [None] * H
    prep_st = [None] * H
    prep_st[0] = emit_prep(0)
    for h in range(H):
        if h >= 1:
            emit_tail(h - 1, prep_st[h - 1])
        if h + 1 < H:
            prep_st[h + 1] = emit_prep(h + 1)
        emit_nloop(h, prep_st[h])
    emit_tail(H - 1, prep_st[H - 1])

    P2_SCR = sScr.n
    P2_V = sV.n
    P2_A = sA.n
    P2_T = sT.n
    sfree("sq128", "ssqa", "ybuf1", "ybuf0", "ygh1", "ygh0", "sd_t",
          "qga1", "qga0", "ega", "deca1", "deca0", "growa", "sbf_t",
          "gs_t1", "gs_t0", "qr_t1", "qr_t0", "qtmp",
          "wgbf1", "wgbf0", "wqbf1", "wqbf0",
          "kda1", "kda0", "xca1", "xca0", "vtm1", "vtm0", "ktm1", "ktm0",
          "kfm1", "kfm0", "ropeB", "ropeA", "S")

    ygf = x1f
    xmid = salloc("xmid", [CH, NCH * HID], BF16)
    wobf = [salloc(f"wobf{i}", [CH, HID], BF16) for i in range(2)]
    xcb = [salloc(f"xcb{i}", [CH, TOK], F32) for i in range(2)]
    at_t = salloc("at_t", [CH, TOK], BF16)
    tstat = salloc("tstat", [CH, TOK], F32)

    def p3_ygin(sync):
        sync.wait_ge(sScr.h, P2_SCR)
        sync.wait_ge(sT.h, P2_T)
        for h in range(H):
            sScr.inc(sync.dma_start(out=ygf[:, h * TOK:(h + 1) * TOK],
                                    in_=ygscr[h * CH:(h + 1) * CH, :]), 16)
    sec("sync", p3_ygin)
    ygin_d = sScr.n

    wo_loaded = []

    def p3_w0(gpsimd):
        gpsimd.wait_ge(sT.h, P2_T)
        for ob in range(2):
            src = dram["wob"][ob * HID:(ob + 1) * HID, :].rearrange(
                "(fc f) o -> f fc o", f=CH)
            d = wobf[ob][:, :].rearrange("f (fc o) -> f fc o", o=CH)
            sW.inc(gpsimd.dma_start(out=d, in_=src), 16)
            wo_loaded.append(sW.n)
    sec("gpsimd", p3_w0)

    x_cb_loaded = []

    def p3_x0(sync):
        sync.wait_ge(sT.h, P2_T)
        for ob in range(2):
            src = dram["x"][:, :].rearrange("(tc t) (obx f) -> obx t tc f",
                                            t=CH, f=CH)[ob]
            d = xcb[ob][:, :].rearrange("t (tc f) -> t tc f", f=CH)
            sIn.inc(sync.dma_start(out=d, in_=src), 16)
            x_cb_loaded.append(sIn.n)
    sec("sync", p3_x0)

    ev_prev = [0]
    add_prev = [0]
    for ob in range(32):
        bi = ob % 2
        wtar = wo_loaded[ob]
        xtar = x_cb_loaded[ob]

        def p3_mm(tensor, ob=ob, bi=bi, wtar=wtar):
            t = tensor
            t.wait_ge(sW.h, wtar)
            t.wait_ge(sV.h, ev_prev[0])
            if ob == 0:
                t.wait_ge(sScr.h, ygin_d)
            for fc in range(NF):
                for th in range(2):
                    sT.inc(t.matmul(
                        pq[:, th * 512:(th + 1) * 512],
                        wobf[bi][:, fc * CH:(fc + 1) * CH],
                        ygf[:, fc * TOK + th * 512: fc * TOK + (th + 1) * 512],
                        start=(fc == 0), stop=(fc == NF - 1)))
        sec("tensor", p3_mm)
        mm_t = sT.n

        def p3_ev(vector, mm_t=mm_t):
            vector.wait_ge(sT.h, mm_t)
            sV.inc(vector.tensor_copy(at_t[:, :], pq[:, :]))
        sec("vector", p3_ev)
        ev_v = sV.n
        ev_prev[0] = ev_v

        if ob + 2 < 32:
            def p3_wn(gpsimd, ob=ob, bi=bi, mm_t=mm_t):
                gpsimd.wait_ge(sT.h, mm_t)
                src = dram["wob"][(ob + 2) * HID:(ob + 3) * HID, :].rearrange(
                    "(fc f) o -> f fc o", f=CH)
                d = wobf[bi][:, :].rearrange("f (fc o) -> f fc o", o=CH)
                sW.inc(gpsimd.dma_start(out=d, in_=src), 16)
                wo_loaded.append(sW.n)
            sec("gpsimd", p3_wn)

        def p3_tr(tensor, ev_v=ev_v):
            t = tensor
            t.wait_ge(sV.h, ev_v)
            t.wait_ge(sV.h, add_prev[0])
            for tc in range(NCH):
                sT.inc(t.transpose(pr[:, tc * 64:(tc + 1) * 64].bitcast(BF16),
                                   at_t[:, tc * CH:(tc + 1) * CH], id16[:, :]))
        sec("tensor", p3_tr)
        tr_t = sT.n

        def p3_add(vector, ob=ob, bi=bi, tr_t=tr_t, xtar=xtar):
            v = vector
            v.wait_ge(sT.h, tr_t)
            v.wait_ge(sIn.h, xtar)
            o3 = xmid[:, :].rearrange("p (tc f) -> p tc f", f=HID)[
                :, :, ob * CH:(ob + 1) * CH]
            i0 = pr[:, 0:512].bitcast(BF16).rearrange("p (tc d) -> p tc d", d=CH)
            i1 = xcb[bi][:, :].rearrange("p (tc d) -> p tc d", d=CH)
            sV.inc(v.tensor_add(o3, i0, i1))
        sec("vector", p3_add)
        add_prev[0] = sV.n

        if ob + 2 < 32:
            def p3_xn(sync, ob=ob, bi=bi):
                sync.wait_ge(sV.h, add_prev[0])
                src = dram["x"][:, :].rearrange("(tc t) (obx f) -> obx t tc f",
                                                t=CH, f=CH)[ob + 2]
                d = xcb[bi][:, :].rearrange("t (tc f) -> t tc f", f=CH)
                sIn.inc(sync.dma_start(out=d, in_=src), 16)
                x_cb_loaded.append(sIn.n)
            sec("sync", p3_xn)

    XMID_V = sV.n

    p3b_r_v = []
    for tc in range(NCH):
        so = 24 * (tc % 2)

        def p3b_s1(vector, tc=tc, so=so):
            v = vector
            if tc == 0:
                v.wait_ge(sV.h, XMID_V)
            sV.inc(v.reduce_sum(stats[:, so + 0:so + 1],
                                xmid[:, tc * HID:(tc + 1) * HID], axis=AX.X))
        sec("vector", p3b_s1)

        def p3b_sq(scalar, tc=tc, so=so):
            if tc == 0:
                scalar.wait_ge(sV.h, XMID_V)
            if tc >= 2:
                scalar.wait_ge(sV.h, p3b_r_v[tc - 2])
            for q4 in range(4):
                sA.inc(scalar.activation(pq[:, :],
                                         xmid[:, tc * HID + q4 * TOK: tc * HID + (q4 + 1) * TOK],
                                         ACTF.Square,
                                         accum_out=stats[:, so + 12 + q4:so + 13 + q4]))
        sec("scalar", p3b_sq)
        sq_a = sA.n

        def p3b_chain(vector, sq_a=sq_a, so=so):
            v = vector
            v.wait_ge(sA.h, sq_a)
            sV.inc(v.tensor_add(stats[:, so + 1:so + 2], stats[:, so + 12:so + 13], stats[:, so + 13:so + 14]))
            v.drain()
            sV.inc(v.tensor_add(stats[:, so + 16:so + 17], stats[:, so + 14:so + 15], stats[:, so + 15:so + 16]))
            v.drain()
            sV.inc(v.tensor_add(stats[:, so + 1:so + 2], stats[:, so + 1:so + 2], stats[:, so + 16:so + 17]))
            v.drain()
            sV.inc(v.tensor_scalar_mul(stats[:, so + 2:so + 3], stats[:, so + 0:so + 1], 1.0 / HID))
            v.drain()
            sV.inc(v.tensor_scalar_mul(stats[:, so + 3:so + 4], stats[:, so + 1:so + 2], 1.0 / HID))
            v.drain()
            sV.inc(v.tensor_mul(stats[:, so + 4:so + 5], stats[:, so + 2:so + 3], stats[:, so + 2:so + 3]))
            v.drain()
            sV.inc(v.tensor_sub(stats[:, so + 5:so + 6], stats[:, so + 3:so + 4], stats[:, so + 4:so + 5]))
            v.drain()
            sV.inc(v.tensor_scalar(stats[:, so + 5:so + 6], stats[:, so + 5:so + 6], 1.0, 1e-5,
                                   ALU.mult, ALU.add))
        sec("vector", p3b_chain)
        ch_v = sV.n

        def p3b_sqrt(scalar, ch_v=ch_v, so=so):
            scalar.wait_ge(sV.h, ch_v)
            sA.inc(scalar.activation(stats[:, so + 6:so + 7], stats[:, so + 5:so + 6], ACTF.Sqrt))
        sec("scalar", p3b_sqrt)
        sq2_a = sA.n

        def p3b_r(vector, tc=tc, sq2_a=sq2_a, so=so):
            v = vector
            v.wait_ge(sA.h, sq2_a)
            sV.inc(v.reciprocal(stats[:, so + 7:so + 8], stats[:, so + 6:so + 7]))
            v.drain()
            sV.inc(v.tensor_mul(stats[:, so + 8:so + 9], stats[:, so + 7:so + 8], stats[:, so + 7:so + 8]))
            v.drain()
            sV.inc(v.tensor_mul(stats[:, so + 8:so + 9], stats[:, so + 8:so + 9], stats[:, so + 5:so + 6]))
            v.drain()
            sV.inc(v.tensor_scalar(stats[:, so + 8:so + 9], stats[:, so + 8:so + 9], -0.5, 1.5,
                                   ALU.mult, ALU.add))
            v.drain()
            sV.inc(v.tensor_mul(r2c[:, tc:tc + 1], stats[:, so + 7:so + 8], stats[:, so + 8:so + 9]))
            v.drain()
            sV.inc(v.tensor_copy(m2c[:, tc:tc + 1], stats[:, so + 2:so + 3]))
            v.drain()
        sec("vector", p3b_r)
        p3b_r_v.append(sV.n)

    for tc in range(NCH):
        so = 24 * (tc % 2)

        def p3c_am(vector, tc=tc, so=so):
            v = vector
            for q4 in range(4):
                sV.inc(v.tensor_scalar(tstat[:, :],
                                       xmid[:, tc * HID + q4 * TOK: tc * HID + (q4 + 1) * TOK],
                                       m2c[:, tc:tc + 1], None, ALU.subtract))
                sV.inc(v.reduce_max(stats[:, 56:57], tstat[:, :], axis=AX.X,
                                    apply_absolute_value=True))
                v.drain()
                sV.inc(v.tensor_mul(stats[:, 57:58], stats[:, 56:57], r2c[:, tc:tc + 1]))
                v.drain()
                if tc == 0 and q4 == 0:
                    sV.inc(v.tensor_copy(stats[:, 58:59], stats[:, 57:58]))
                else:
                    sV.inc(v.tensor_max(stats[:, 58:59], stats[:, 58:59], stats[:, 57:58]))
                v.drain()
        sec("vector", p3c_am)
    amcol_v = sV.n

    def p3b_amtr(tensor):
        tensor.wait_ge(sV.h, amcol_v)
        sT.inc(tensor.transpose(ps1[0:1, 0:CH], stats[:, 58:59], id32[:, :]))
    sec("tensor", p3b_amtr)
    amtr_t = sT.n

    def p3b_amfin(vector, amtr_t=amtr_t):
        vector.wait_ge(sT.h, amtr_t)
        sV.inc(vector.reduce_max(amaxsb[0:1, 0:1], ps1[0:1, 0:CH], axis=AX.X))
    sec("vector", p3b_amfin)
    am_v = sV.n

    def p3b_amout(sync):
        sync.wait_ge(sV.h, am_v)
        sScr.inc(sync.dma_start(out=am_in, in_=amaxsb[0:1, 0:2]), 16)
    sec("sync", p3b_amout)
    am_dma = sScr.n

    def p3b_ar(gpsimd):
        gpsimd.wait_ge(sScr.h, am_dma)
        sG.inc(gpsimd.collective_compute(
            "AllReduce", ALU.max,
            replica_groups=[[0, 1, 2, 3, 4, 5, 6, 7]],
            ins=[am_in], outs=[am_out]))
    sec("gpsimd", p3b_ar)
    ar_g = sG.n

    def p3b_amin(sync):
        sync.wait_ge(sG.h, ar_g)
        sScr.inc(sync.dma_start(out=amaxsb[0:1, 2:4], in_=am_out), 16)
    sec("sync", p3b_amin)
    amin_d = sScr.n

    def p3b_scale(vector, amin_d=amin_d):
        v = vector
        v.wait_ge(sScr.h, amin_d)
        sV.inc(v.tensor_scalar(amaxsb[0:1, 2:4], amaxsb[0:1, 2:4], 1e-12, None, ALU.max))
        v.drain()
        sV.inc(v.reciprocal(sclrow[0:1, 0:2], amaxsb[0:1, 2:4]))
        v.drain()
        sV.inc(v.tensor_scalar_mul(sclrow[0:1, 0:2], sclrow[0:1, 0:2], 224.0))
        sV.inc(v.tensor_mul(sclrow[0:1, 2:3], amaxsb[0:1, 2:3], amaxsb[0:1, 3:4]))
        v.drain()
        sV.inc(v.tensor_scalar_mul(sclrow[0:1, 2:3], sclrow[0:1, 2:3], 1.0 / 50176.0))
        v.drain()
    sec("vector", p3b_scale)
    scl_v = sV.n

    def p3b_sbc(tensor, scl_v=scl_v):
        tensor.wait_ge(sV.h, scl_v)
        sT.inc(tensor.matmul(ps1[:, 0:4], ones1[0:1, :], sclrow[0:1, 0:4],
                             start=True, stop=True))
    sec("tensor", p3b_sbc)
    sbc_t = sT.n

    def p3b_sbc2(vector, sbc_t=sbc_t):
        v = vector
        v.wait_ge(sT.h, sbc_t)
        sV.inc(v.tensor_copy(sclbc[:, :], ps1[:, 0:4]))
        v.drain()
        sV.inc(v.tensor_scalar(sc2[:, :], r2c[:, :], sclbc[:, 0:1], None, ALU.mult))
        v.drain()
        sV.inc(v.tensor_mul(nb2[:, :], m2c[:, :], sc2[:, :]))
        v.drain()
        sV.inc(v.tensor_scalar_mul(nb2[:, :], nb2[:, :], -1.0))
        v.drain()
    sec("vector", p3b_sbc2)
    SCL_V = sV.n
    P3_T = sT.n

    sfree("tstat", "at_t", "xcb1", "xcb0", "wobf1", "wobf0")

    x2tmp = salloc("x2tmp", [CH, HID], FP8)
    x2f = x1f[:, 0:NF * TOK // 2].bitcast(FP8)
    wq8 = x1f[:, NF * TOK // 2:NF * TOK].bitcast(FP8)
    fwtmp = [salloc(f"fwtmp{i}", [CH, 1024], F32) for i in range(2)]
    f1_t = salloc("f1_t", [CH, TOK], BF16)
    otA = salloc("otA", [CH, TOK], F32)

    for tc in range(NCH):
        def p5_x2(scalar, tc=tc):
            s = scalar
            if tc == 0:
                s.wait_ge(sV.h, SCL_V)
                s.wait_ge(sT.h, P3_T)
            else:
                s.wait_ge(sV.h, sV.n)
            sA.inc(s.activation(x2tmp[:, :], xmid[:, tc * HID:(tc + 1) * HID],
                                ACTF.Identity, bias=nb2[:, tc:tc + 1], scale=sc2[:, tc:tc + 1]))
        sec("scalar", p5_x2)
        x2_a = sA.n

        for g4 in range(8):
            def p5_tr(tensor, g4=g4, x2_a=x2_a):
                t = tensor
                t.wait_ge(sA.h, x2_a)
                t.wait_ge(sV.h, sV.n)
                for i in range(4):
                    fc = g4 * 4 + i
                    sT.inc(t.transpose(
                        ps2[:, i * 64:(i + 1) * 64].bitcast(FP8)[:, 0:256:2],
                        x2tmp[:, fc * CH:(fc + 1) * CH], id8[:, :]))
            sec("tensor", p5_tr)
            tr_t = sT.n

            def p5_te(vector, tc=tc, g4=g4, tr_t=tr_t):
                vector.wait_ge(sT.h, tr_t)
                dst = x2f.rearrange("p (fc t) -> p fc t", t=TOK)[
                    :, g4 * 4:(g4 + 1) * 4, tc * CH:(tc + 1) * CH]
                src = ps2[:, 0:256].bitcast(FP8)[:, 0:1024:2].rearrange(
                    "p (fc t) -> p fc t", t=CH)
                sV.inc(vector.tensor_copy(dst, src))
            sec("vector", p5_te)

    X2_V = sV.n

    for quarter in range(4):
        fw_loaded = []

        def p5_fw0(gpsimd, quarter=quarter):
            gpsimd.wait_ge(sV.h, X2_V if quarter == 0 else sV.n)
            for fc in range(2):
                sW.inc(gpsimd.dma_start(
                    out=fwtmp[fc][:, :],
                    in_=dram["fwT"][fc * CH:(fc + 1) * CH,
                                    quarter * 1024:(quarter + 1) * 1024]), 16)
                fw_loaded.append(sW.n)
        sec("gpsimd", p5_fw0)

        for fc in range(NF):
            bi = fc % 2
            wtar = fw_loaded[fc]

            def p5_q(vector, fc=fc, bi=bi, wtar=wtar):
                vector.wait_ge(sW.h, wtar)
                sV.inc(vector.tensor_scalar(wq8[:, fc * 1024:(fc + 1) * 1024],
                                            fwtmp[bi][:, :], sclbc[:, 1:2], None,
                                            ALU.mult))
            sec("vector", p5_q)
            q_v = sV.n

            if fc + 2 < NF:
                def p5_fwn(gpsimd, fc=fc, bi=bi, quarter=quarter, q_v=q_v):
                    gpsimd.wait_ge(sV.h, q_v)
                    sW.inc(gpsimd.dma_start(
                        out=fwtmp[bi][:, :],
                        in_=dram["fwT"][(fc + 2) * CH:(fc + 3) * CH,
                                        quarter * 1024:(quarter + 1) * 1024]), 16)
                    fw_loaded.append(sW.n)
                sec("gpsimd", p5_fwn)

        WQ_V = sV.n

        f1_prev = [0]
        oadd_prev = [0]
        for obq in range(8):
            ob = quarter * 8 + obq

            def p5_mm(tensor, obq=obq, WQ_V=WQ_V):
                t = tensor
                t.wait_ge(sV.h, WQ_V if obq == 0 else f1_prev[0])
                for fc in range(NF):
                    for th in range(2):
                        sT.inc(t.matmul(
                            pq[:, th * 512:(th + 1) * 512],
                            wq8[:, fc * 1024 + obq * CH: fc * 1024 + (obq + 1) * CH],
                            x2f[:, fc * TOK + th * 512: fc * TOK + (th + 1) * 512],
                            start=(fc == 0), stop=(fc == NF - 1)))
            sec("tensor", p5_mm)
            mm_t = sT.n

            def p5_f1(vector, ob=ob, mm_t=mm_t):
                vector.wait_ge(sT.h, mm_t)
                sV.inc(vector.tensor_scalar(f1_t[:, :], pq[:, :], sclbc[:, 2:3],
                                            fbias[:, ob:ob + 1], ALU.mult, ALU.add))
            sec("vector", p5_f1)
            f1_v = sV.n
            f1_prev[0] = f1_v

            def p5_otr(tensor, f1_v=f1_v):
                t = tensor
                t.wait_ge(sV.h, f1_v)
                t.wait_ge(sV.h, oadd_prev[0])
                for tc in range(NCH):
                    sT.inc(t.transpose(pr[:, tc * 64:(tc + 1) * 64].bitcast(BF16),
                                       f1_t[:, tc * CH:(tc + 1) * CH], id16[:, :]))
            sec("tensor", p5_otr)
            otr_t = sT.n

            def p5_oadd(vector, ob=ob, otr_t=otr_t):
                v = vector
                v.wait_ge(sT.h, otr_t)
                if sOut.n >= 128:
                    v.wait_ge(sOut.h, sOut.n - 128)
                o3 = otA[:, :].rearrange("p (tc d) -> p tc d", d=CH)
                i0 = pr[:, 0:512].bitcast(BF16).rearrange("p (tc d) -> p tc d", d=CH)
                i1 = xmid[:, :].rearrange("p (tc f) -> p tc f", f=HID)[
                    :, :, ob * CH:(ob + 1) * CH]
                sV.inc(v.tensor_add(o3, i0, i1))
            sec("vector", p5_oadd)
            oadd_v = sV.n
            oadd_prev[0] = oadd_v

            def p5_out(sync, ob=ob, oadd_v=oadd_v):
                sync.wait_ge(sV.h, oadd_v)
                for tc in range(NCH):
                    sOut.inc(sync.dma_start(
                        out=y_out[tc * CH:(tc + 1) * CH, ob * CH:(ob + 1) * CH],
                        in_=otA[:, tc * CH:(tc + 1) * CH]), 16)
            sec("sync", p5_out)

    def fin(sync):
        sync.wait_ge(sOut.h, sOut.n)
    sec("sync", fin)

    block_cm.__exit__(None, None, None)
    for nm in reversed(list(LIVE)):
        LIVE.pop(nm).__exit__(None, None, None)
    for nm in reversed(list(pcm)):
        pcm.pop(nm).__exit__(None, None, None)
    return nc


_NC_CACHE = None


def _consts():
    j = np.arange(CH)
    tri = (j[:, None] <= j[None, :]).astype(np.float32)
    ones128 = np.ones((CH, CH), np.float32)
    ones1 = np.ones((1, CH), np.float32)
    rotm = np.zeros((CH, CH), np.float32)
    for d in range(CH):
        rotm[(d + 64) % CH, d] = 1.0
    maskneg = np.where(j[:, None] <= j[None, :], 0.0, -1e9).astype(np.float32)
    ident = np.eye(CH, dtype=np.float32)
    return dict(
        tri=tri, ones128=ones128, ones1=ones1,
        rot=rotm.astype(NPBF), id16=ident.astype(NPBF), id32=ident,
        id8=ident.astype(ml_dtypes.float8_e4m3), maskneg=maskneg,
        negones1=-ones1)


def kernel(L0_attn_hidden_states, L0_attn_Wq, L0_attn_Wk, L0_attn_Wv, L0_attn_Wo,
           L0_attn_Wg, L0_attn_Win, L0_attn_dt_bias, L0_attn_A_log_bias,
           L0_attn_gnorm_w, L0_ffn_weight, L0_ffn_bias):
    global _NC_CACHE
    x = np.ascontiguousarray(L0_attn_hidden_states)

    def blk(w, nb):
        ofd, f = w.shape
        d = ofd // nb
        wt = np.ascontiguousarray(w.T)
        out = np.empty((nb * f, d), np.float32)
        for b_ in range(nb):
            out[b_ * f:(b_ + 1) * f, :] = wt[:, b_ * d:(b_ + 1) * d]
        return out

    wqb = blk(L0_attn_Wq, 32)
    wkb = blk(L0_attn_Wk, 8)
    wvb = blk(L0_attn_Wv, 8)
    wgb = blk(L0_attn_Wg, 32)
    wob = blk(L0_attn_Wo, 32)
    winT = np.ascontiguousarray(L0_attn_Win.T)
    fwT = np.ascontiguousarray(L0_ffn_weight.T)

    invf = (1.0 / 10000.0 ** (np.arange(0, D, 2, dtype=np.float32) / np.float32(D))).astype(np.float32)
    posL = np.arange(L, dtype=np.float32)
    ang = posL[:, None] * invf[None, :]
    cosL = np.concatenate([np.cos(ang), np.cos(ang)], axis=1).astype(np.float32)
    sinL = np.concatenate([np.sin(ang), np.sin(ang)], axis=1).astype(np.float32)
    sgn = np.ones(D, np.float32)
    sgn[:64] = -1.0
    sin2L = sinL * sgn[None, :]

    consts = _consts()
    if _NC_CACHE is None:
        _NC_CACHE = build_nc()
    nc = _NC_CACHE

    in_maps = []
    for c in range(8):
        b_ = c // 4
        lc = c % 4
        t0 = lc * TOK
        msk = np.broadcast_to((np.arange(4) < lc).astype(np.float32), (CH, 4)).copy()
        im = dict(
            x=np.ascontiguousarray(x[b_, t0:t0 + TOK]),
            wqb=wqb, wkb=wkb, wvb=wvb, wgb=wgb, winT=winT, wob=wob, fwT=fwT,
            fwsl=np.ascontiguousarray(fwT[512 * c:512 * (c + 1), :]),
            dtb=L0_attn_dt_bias.reshape(32, 1).astype(np.float32),
            alog=L0_attn_A_log_bias.reshape(32, 1).astype(np.float32),
            gnormc=L0_attn_gnorm_w.reshape(D, 1).astype(np.float32),
            fbias=np.ascontiguousarray(L0_ffn_bias.reshape(32, D).T).astype(np.float32),
            cosfm=np.ascontiguousarray(cosL[t0:t0 + TOK].T).astype(NPBF),
            sin2fm=np.ascontiguousarray(sin2L[t0:t0 + TOK].T).astype(NPBF),
            mskd=msk, ommd=np.ascontiguousarray(1.0 - msk),
            **consts,
        )
        in_maps.append(im)

    res = run_bass_kernel_spmd(nc, in_maps, core_ids=list(range(8)))
    out = np.empty((B, L, HID), np.float32)
    for c in range(8):
        out[c // 4, (c % 4) * TOK:(c % 4 + 1) * TOK] = res.results[c]["y"]
    return out


# version 21
# speedup vs baseline: 1.0238x; 1.0238x over previous
import numpy as np
import ml_dtypes

import concourse.bass as bass
import concourse.mybir as mybir
from concourse.bass_utils import run_bass_kernel_spmd

F32 = mybir.dt.float32
BF16 = mybir.dt.bfloat16
FP8 = mybir.dt.float8e4
ALU = mybir.AluOpType
ACTF = mybir.ActivationFunctionType
AX = mybir.AxisListType

NPBF = ml_dtypes.bfloat16

TOK = 1024
NCH = 8
CH = 128
HID = 4096
NF = 32
H = 32
KVH = 8
D = 128
B, L = 2, 4096


class Sem:
    def __init__(self, h):
        self.h = h
        self.n = 0

    def inc(self, instr, k=1):
        instr.then_inc(self.h, k)
        self.n += k
        return self.n


def build_nc():
    nc = bass.Bass("TRN2", target_bir_lowering=False, debug=False)

    dram = {}
    for name, shape, dt in [
        ("x", [TOK, HID], F32),
        ("wqb", [H * HID, D], F32),
        ("wkb", [KVH * HID, D], F32),
        ("wvb", [KVH * HID, D], F32),
        ("wgb", [H * HID, D], F32),
        ("winT", [HID, 32], F32),
        ("wob", [32 * HID, D], F32),
        ("fwT", [HID, HID], F32),
        ("fwsl", [512, HID], F32),
        ("dtb", [32, 1], F32),
        ("alog", [32, 1], F32),
        ("gnormc", [D, 1], F32),
        ("fbias", [D, 32], F32),
        ("cosfm", [D, TOK], BF16),
        ("sin2fm", [D, TOK], BF16),
        ("mskd", [CH, 4], F32),
        ("ommd", [CH, 4], F32),
        ("tri", [CH, CH], F32),
        ("ones128", [CH, CH], F32),
        ("ones1", [1, CH], F32),
        ("rot", [CH, CH], BF16),
        ("id16", [CH, CH], BF16),
        ("id32", [CH, CH], F32),
        ("id8", [CH, CH], FP8),
        ("maskneg", [CH, CH], F32),
        ("negones1", [1, CH], F32),
    ]:
        dram[name] = nc.dram_tensor(name, shape, dt, kind="ExternalInput").ap()

    y_out = nc.dram_tensor("y", [TOK, HID], F32, kind="ExternalOutput").ap()

    kvscr = nc.dram_tensor("kvscr", [KVH * 3 * CH, TOK], BF16).ap()
    ygscr = nc.dram_tensor("ygscr", [H * CH, TOK], BF16).ap()
    ag_in = nc.dram_tensor("ag_in", [CH, 4128], F32).ap()
    ag_out = nc.dram_tensor("ag_out", [4 * CH, 4128], F32).ap()
    am_in = nc.dram_tensor("am_in", [1, 2], F32).ap()
    am_out = nc.dram_tensor("am_out", [1, 2], F32).ap()

    LIVE = {}

    def salloc(name, shape, dt):
        cm = nc.sbuf_tensor(name, shape, dt)
        t = cm.__enter__()
        LIVE[name] = cm
        return t

    def sfree(*names):
        for nm in names:
            LIVE.pop(nm).__exit__(None, None, None)

    tri = salloc("triS", [CH, CH], F32)
    ones128 = salloc("ones128S", [CH, CH], F32)
    ones1 = salloc("ones1S", [1, CH], F32)
    rot = salloc("rotS", [CH, CH], BF16)
    id16 = salloc("id16S", [CH, CH], BF16)
    id32 = salloc("id32S", [CH, CH], F32)
    id8 = salloc("id8S", [CH, CH], FP8)
    maskneg = salloc("masknegS", [CH, CH], F32)
    negones1 = salloc("negones1S", [1, CH], F32)
    cosfm = salloc("cosfmS", [D, TOK], BF16)
    sin2fm = salloc("sin2fmS", [D, TOK], BF16)
    gnormc = salloc("gnormcS", [D, 1], F32)
    dtbc = salloc("dtbcS", [32, 1], F32)
    alog = salloc("alogS", [32, 1], F32)
    nega = salloc("negaS", [32, 1], F32)
    ndtb = salloc("ndtbS", [32, 1], F32)
    fbias = salloc("fbiasS", [D, 32], F32)
    mskd = salloc("mskdS", [CH, 4], F32)
    ommd = salloc("ommdS", [CH, 4], F32)

    dttm = salloc("dttm", [CH, NCH * 32], F32)
    gtm = salloc("gtm", [CH, NCH * 32], F32)
    Gcol = salloc("Gcol", [CH, NCH * 32], F32)
    ecol = salloc("ecol", [CH, NCH * 32], F32)
    cdec = salloc("cdec", [CH, NCH * 32], F32)
    abc = salloc("abc", [CH, 32], F32)
    stats = salloc("stats", [CH, 64], F32)
    m2c = salloc("m2c", [CH, 8], F32)
    r2c = salloc("r2c", [CH, 8], F32)
    sc2 = salloc("sc2", [CH, 8], F32)
    nb2 = salloc("nb2", [CH, 8], F32)
    sclbc = salloc("sclbc", [CH, 4], F32)
    amaxsb = salloc("amaxsb", [1, 4], F32)
    sclrow = salloc("sclrow", [1, 4], F32)
    grow = salloc("grow", [1, CH], F32)
    small = salloc("small", [CH, 64], F32)
    x1f = salloc("x1f", [CH, NF * TOK], BF16)
    S = salloc("S", [CH, H * D], F32)
    ropeA = salloc("ropeA", [CH, TOK], BF16)
    ropeB = salloc("ropeB", [CH, TOK], BF16)
    kfm = [salloc(f"kfm{i}", [CH, TOK], BF16) for i in range(2)]
    ktm = [salloc(f"ktm{i}", [CH, TOK], BF16) for i in range(2)]
    vtm = [salloc(f"vtm{i}", [CH, TOK], BF16) for i in range(2)]
    xca = [salloc(f"xca{i}", [CH, TOK], BF16) for i in range(2)]
    kda = [salloc(f"kda{i}", [CH, TOK], BF16) for i in range(2)]
    ktmp = salloc("ktmp", [CH, TOK], BF16)
    dthm = salloc("dthm", [32, TOK], F32)
    ghm = salloc("ghm", [32, TOK], F32)
    winbf = salloc("winbf", [CH, NF * 32], BF16)
    wkbf = [salloc(f"wkbf{i}", [CH, HID], BF16) for i in range(2)]
    wvbf = [salloc(f"wvbf{i}", [CH, HID], BF16) for i in range(2)]

    pcm = {}
    def palloc(name, shape):
        cm = nc.psum_tensor(name, shape, F32)
        t = cm.__enter__()
        pcm[name] = cm
        return t
    pq = palloc("pq", [CH, 1024])
    pg = palloc("pg", [CH, 1024])
    pr = palloc("pr", [CH, 1024])
    ps1 = palloc("ps1", [CH, 512])
    ps2 = palloc("ps2", [CH, 512])

    sems = {}
    for s in ["sIn", "sW", "sScr", "sT", "sV", "sA", "sG", "sOut"]:
        sems[s] = Sem(nc.semaphore(s).__enter__())
    sIn, sW, sScr, sT, sV, sA, sG, sOut = (sems[k] for k in
        ["sIn", "sW", "sScr", "sT", "sV", "sA", "sG", "sOut"])

    block_cm = nc.Block()
    block = block_cm.__enter__()

    def colbcast(t, h):
        return t[:, :].rearrange("p (n c) -> p n c", c=32)[:, :, h:h + 1].broadcast_to(
            [CH, NCH, CH])

    def sec(name, fn):
        getattr(block, name)(fn)

    xt = salloc("xt", [CH, HID], F32)
    xt2 = salloc("xt2", [CH, HID], F32)
    x1tm = salloc("x1tm", [CH, HID], BF16)
    fwt_t = salloc("fwt_t", [CH, 256], F32)

    def c_loads(sync):
        for nm, dst in [("tri", tri), ("ones128", ones128), ("ones1", ones1),
                        ("rot", rot), ("id16", id16), ("id32", id32), ("id8", id8),
                        ("maskneg", maskneg), ("cosfm", cosfm), ("sin2fm", sin2fm),
                        ("gnormc", gnormc), ("dtb", dtbc), ("alog", alog),
                        ("negones1", negones1),
                        ("fbias", fbias), ("mskd", mskd), ("ommd", ommd)]:
            sIn.inc(sync.dma_start(out=dst[:, :], in_=dram[nm]), 16)
    sec("sync", c_loads)
    CONSTS_IN = sIn.n

    def c_nega(scalar):
        scalar.wait_ge(sIn.h, CONSTS_IN)
        sA.inc(scalar.activation(nega[:, :], alog[:, :], ACTF.Exp))
    sec("scalar", c_nega)
    NEGA_A = sA.n

    def c_nega2(vector):
        vector.wait_ge(sA.h, NEGA_A)
        sV.inc(vector.tensor_scalar_mul(ndtb[:, :], dtbc[:, :], -1.0))
    sec("vector", c_nega2)

    x_loaded = []

    def p0_xload(sync):
        for tc0 in range(2):
            x_loaded.append(sIn.inc(sync.dma_start(
                out=(xt if tc0 % 2 == 0 else xt2)[:, :],
                in_=dram["x"][tc0 * CH:(tc0 + 1) * CH, :]), 16))
    sec("sync", p0_xload)

    p0_x1_a = []
    p0_tr_t = []
    p0_rstd_v = []

    for tc in range(NCH):
        buf = xt if tc % 2 == 0 else xt2
        ld = x_loaded[tc]
        so = 24 * (tc % 2)

        def p0_stats(vector, buf=buf, ld=ld, so=so):
            vector.wait_ge(sIn.h, ld)
            sV.inc(vector.reduce_sum(stats[:, so + 0:so + 1], buf[:, :], axis=AX.X))
        sec("vector", p0_stats)

        def p0_sq(scalar, buf=buf, ld=ld, so=so, tc=tc):
            scalar.wait_ge(sIn.h, ld)
            if tc >= 2:
                scalar.wait_ge(sV.h, p0_rstd_v[tc - 2])
            for q4 in range(4):
                sA.inc(scalar.activation(pq[:, :], buf[:, q4 * TOK:(q4 + 1) * TOK],
                                         ACTF.Square,
                                         accum_out=stats[:, so + 12 + q4:so + 13 + q4]))
        sec("scalar", p0_sq)
        sq_a = sA.n

        def p0_chain(vector, sq_a=sq_a, so=so):
            v = vector
            v.wait_ge(sA.h, sq_a)
            sV.inc(v.tensor_add(stats[:, so + 1:so + 2], stats[:, so + 12:so + 13], stats[:, so + 13:so + 14]))
            v.drain()
            sV.inc(v.tensor_add(stats[:, so + 16:so + 17], stats[:, so + 14:so + 15], stats[:, so + 15:so + 16]))
            v.drain()
            sV.inc(v.tensor_add(stats[:, so + 1:so + 2], stats[:, so + 1:so + 2], stats[:, so + 16:so + 17]))
            v.drain()
            sV.inc(v.tensor_scalar_mul(stats[:, so + 2:so + 3], stats[:, so + 0:so + 1], 1.0 / HID))
            v.drain()
            sV.inc(v.tensor_scalar_mul(stats[:, so + 3:so + 4], stats[:, so + 1:so + 2], 1.0 / HID))
            v.drain()
            sV.inc(v.tensor_mul(stats[:, so + 4:so + 5], stats[:, so + 2:so + 3], stats[:, so + 2:so + 3]))
            v.drain()
            sV.inc(v.tensor_sub(stats[:, so + 5:so + 6], stats[:, so + 3:so + 4], stats[:, so + 4:so + 5]))
            v.drain()
            sV.inc(v.tensor_scalar(stats[:, so + 5:so + 6], stats[:, so + 5:so + 6], 1.0, 1e-5,
                                   ALU.mult, ALU.add))
            v.drain()
        sec("vector", p0_chain)
        ch_v = sV.n

        def p0_sqrt(scalar, ch_v=ch_v, so=so):
            scalar.wait_ge(sV.h, ch_v)
            sA.inc(scalar.activation(stats[:, so + 6:so + 7], stats[:, so + 5:so + 6], ACTF.Sqrt))
        sec("scalar", p0_sqrt)
        sq2_a = sA.n

        def p0_rstd(vector, sq2_a=sq2_a, so=so):
            v = vector
            v.wait_ge(sA.h, sq2_a)
            sV.inc(v.reciprocal(stats[:, so + 7:so + 8], stats[:, so + 6:so + 7]))
            v.drain()
            sV.inc(v.tensor_mul(stats[:, so + 8:so + 9], stats[:, so + 7:so + 8], stats[:, so + 7:so + 8]))
            v.drain()
            sV.inc(v.tensor_mul(stats[:, so + 8:so + 9], stats[:, so + 8:so + 9], stats[:, so + 5:so + 6]))
            v.drain()
            sV.inc(v.tensor_scalar(stats[:, so + 8:so + 9], stats[:, so + 8:so + 9], -0.5, 1.5,
                                   ALU.mult, ALU.add))
            v.drain()
            sV.inc(v.tensor_mul(stats[:, so + 9:so + 10], stats[:, so + 7:so + 8], stats[:, so + 8:so + 9]))
            v.drain()
            sV.inc(v.tensor_mul(stats[:, so + 10:so + 11], stats[:, so + 2:so + 3], stats[:, so + 9:so + 10]))
            v.drain()
            sV.inc(v.tensor_scalar_mul(stats[:, so + 11:so + 12], stats[:, so + 10:so + 11], -1.0))
            v.drain()
        sec("vector", p0_rstd)
        p0_rstd_v.append(sV.n)
        rstd_v = sV.n

        def p0_x1(scalar, buf=buf, rstd_v=rstd_v, so=so, tc=tc):
            scalar.wait_ge(sV.h, rstd_v)
            if tc > 0:
                scalar.wait_ge(sT.h, p0_tr_t[tc - 1])
            sA.inc(scalar.activation(x1tm[:, :], buf[:, :], ACTF.Identity,
                                     bias=stats[:, so + 11:so + 12],
                                     scale=stats[:, so + 9:so + 10]))
        sec("scalar", p0_x1)
        x1_a = sA.n
        p0_x1_a.append(x1_a)

        for g4 in range(8):
            pst = ps1 if g4 % 2 == 0 else ps2

            def p0_tr(tensor, g4=g4, pst=pst, x1_a=x1_a):
                tensor.wait_ge(sA.h, x1_a)
                tensor.wait_ge(sV.h, sV.n)
                for i in range(4):
                    fc = g4 * 4 + i
                    sT.inc(tensor.transpose(pst[:, i * 64:(i + 1) * 64].bitcast(BF16),
                                            x1tm[:, fc * CH:(fc + 1) * CH],
                                            id16[:, :]))
            sec("tensor", p0_tr)
            tr_t = sT.n

            def p0_evac(vector, g4=g4, pst=pst, tr_t=tr_t, tc=tc):
                vector.wait_ge(sT.h, tr_t)
                dst = x1f[:, :].rearrange("p (fc t) -> p fc t", t=TOK)[
                    :, g4 * 4:(g4 + 1) * 4, tc * CH:(tc + 1) * CH]
                src = pst[:, 0:256].bitcast(BF16).rearrange("p (fc t) -> p fc t", t=CH)
                sV.inc(vector.tensor_copy(dst, src))
            sec("vector", p0_evac)
        p0_tr_t.append(sT.n)

        if tc + 2 < NCH:
            def p0_next(sync, tc=tc):
                sync.wait_ge(sA.h, p0_x1_a[tc])
                x_loaded.append(sIn.inc(sync.dma_start(
                    out=(xt if tc % 2 == 0 else xt2)[:, :],
                    in_=dram["x"][(tc + 2) * CH:(tc + 3) * CH, :]), 16))
            sec("sync", p0_next)

    def p0_wam(sync):
        sScr.inc(sync.dma_start(out=fwt_t[:, :], in_=dram["fwsl"][0:CH, 0:256]), 16)
    sec("sync", p0_wam)

    for i in range(64):
        ld = sScr.n

        def p0_wred(vector, i=i, ld=ld):
            v = vector
            v.wait_ge(sScr.h, ld)
            sV.inc(v.reduce_max(stats[:, 48:49], fwt_t[:, :],
                                axis=AX.X, apply_absolute_value=True))
            v.drain()
            if i == 0:
                sV.inc(v.tensor_copy(stats[:, 54:55], stats[:, 48:49]))
            else:
                sV.inc(v.tensor_max(stats[:, 54:55], stats[:, 54:55], stats[:, 48:49]))
            v.drain()
        sec("vector", p0_wred)
        red_v = sV.n

        if i + 1 < 64:
            def p0_wnext(sync, i=i, red_v=red_v):
                sync.wait_ge(sV.h, red_v)
                r, cq = divmod(i + 1, 16)
                sScr.inc(sync.dma_start(out=fwt_t[:, :],
                                        in_=dram["fwsl"][r * CH:(r + 1) * CH,
                                                         cq * 256:(cq + 1) * 256]), 16)
            sec("sync", p0_wnext)
    wcol_v = sV.n

    def p0_wtr(tensor):
        tensor.wait_ge(sV.h, wcol_v)
        tensor.wait_ge(sV.h, sV.n)
        sT.inc(tensor.transpose(ps1[0:1, 0:CH], stats[:, 54:55], id32[:, :]))
    sec("tensor", p0_wtr)
    wtr_t = sT.n

    def p0_wfin(vector, wtr_t=wtr_t):
        vector.wait_ge(sT.h, wtr_t)
        sV.inc(vector.reduce_max(amaxsb[0:1, 1:2], ps1[0:1, 0:CH], axis=AX.X))
    sec("vector", p0_wfin)

    P0_T, P0_V, P0_A = sT.n, sV.n, sA.n
    sfree("fwt_t", "x1tm", "xt2", "xt")


    def p1_win(gpsimd):
        gpsimd.wait_ge(sA.h, P0_A)
        gpsimd.wait_ge(sV.h, P0_V)
        gpsimd.wait_ge(sT.h, P0_T)
        src = dram["winT"].rearrange("(fc f) o -> f fc o", f=CH)
        dst = winbf[:, :].rearrange("f (fc o) -> f fc o", o=32)
        sW.inc(gpsimd.dma_start(out=dst, in_=src), 16)
    sec("gpsimd", p1_win)
    win_w = sW.n
    X1_ALL = P0_V

    def p1_dtmm(tensor):
        tensor.wait_ge(sW.h, win_w)
        tensor.wait_ge(sV.h, X1_ALL)
        for fc in range(NF):
            for th in range(2):
                sT.inc(tensor.matmul(
                    pq[0:32, th * 512:(th + 1) * 512],
                    winbf[:, fc * 32:(fc + 1) * 32],
                    x1f[:, fc * TOK + th * 512: fc * TOK + (th + 1) * 512],
                    start=(fc == 0), stop=(fc == NF - 1)))
    sec("tensor", p1_dtmm)
    dt_t = sT.n

    def p1_dthm(scalar):
        s = scalar
        s.wait_ge(sT.h, dt_t)
        s.wait_ge(sV.h, sV.n)
        sA.inc(s.activation(ghm[:, :], pq[0:32, :], ACTF.Sigmoid,
                            bias=ndtb[:, :], scale=-1.0))
        sA.inc(s.activation(dthm[:, :], ghm[:, :], ACTF.Ln))
    sec("scalar", p1_dthm)
    dthm_a = sA.n

    def p1_ghm(vector):
        v = vector
        v.wait_ge(sA.h, dthm_a)
        sV.inc(v.tensor_scalar(ghm[:, :], dthm[:, :], nega[:, :], None, ALU.mult))
        sV.inc(v.tensor_scalar_mul(dthm[:, :], dthm[:, :], -1.0))
    sec("vector", p1_ghm)
    ghm_v = sV.n

    for n in range(NCH):
        def p1_tdg(tensor, n=n):
            tensor.wait_ge(sV.h, ghm_v)
            tensor.wait_ge(sV.h, sV.n)
            sT.inc(tensor.transpose(ps1[:, 0:32], dthm[:, n * CH:(n + 1) * CH], id32[0:32, 0:32]))
            sT.inc(tensor.transpose(ps1[:, 32:64], ghm[:, n * CH:(n + 1) * CH], id32[0:32, 0:32]))
        sec("tensor", p1_tdg)
        tdg_t = sT.n

        def p1_edg(vector, n=n, tdg_t=tdg_t):
            vector.wait_ge(sT.h, tdg_t)
            sV.inc(vector.tensor_copy(dttm[:, n * 32:(n + 1) * 32], ps1[:, 0:32]))
            sV.inc(vector.tensor_copy(gtm[:, n * 32:(n + 1) * 32], ps1[:, 32:64]))
        sec("vector", p1_edg)

    GTM_V = sV.n

    def p1_abc0(vector):
        sV.inc(vector.memset(abc[:, :], 1.0))
    sec("vector", p1_abc0)

    for n in range(NCH):
        def p1_gmm(tensor, n=n):
            tensor.wait_ge(sV.h, GTM_V)
            tensor.wait_ge(sV.h, sV.n)
            tensor.wait_ge(sA.h, sA.n)
            sT.inc(tensor.matmul(ps2[:, 0:32], tri[:, :],
                                 gtm[:, n * 32:(n + 1) * 32], start=True, stop=True))
            sT.inc(tensor.matmul(ps2[:, 32:64], ones128[:, :],
                                 gtm[:, n * 32:(n + 1) * 32], start=True, stop=True))
        sec("tensor", p1_gmm)
        gmm_t = sT.n

        def p1_gev(vector, n=n, gmm_t=gmm_t):
            v = vector
            v.wait_ge(sT.h, gmm_t)
            sV.inc(v.tensor_copy(Gcol[:, n * 32:(n + 1) * 32], ps2[:, 0:32]))
            v.drain()
            sV.inc(v.tensor_sub(small[:, 0:32], ps2[:, 32:64], Gcol[:, n * 32:(n + 1) * 32]))
            v.drain()
        sec("vector", p1_gev)
        gev_v = sV.n

        def p1_gex(scalar, n=n, gev_v=gev_v):
            s = scalar
            s.wait_ge(sV.h, gev_v)
            sA.inc(s.activation(ecol[:, n * 32:(n + 1) * 32], small[:, 0:32], ACTF.Exp))
            sA.inc(s.activation(cdec[:, n * 32:(n + 1) * 32], ps2[:, 32:64], ACTF.Exp))
        sec("scalar", p1_gex)
        gex_a = sA.n

        def p1_abc(vector, n=n, gex_a=gex_a):
            vector.wait_ge(sA.h, gex_a)
            vector.drain()
            sV.inc(vector.tensor_mul(abc[:, :], abc[:, :], cdec[:, n * 32:(n + 1) * 32]))
        sec("vector", p1_abc)

    def p1_szero(vector):
        sV.inc(vector.memset(S[:, :], 0.0))
    sec("vector", p1_szero)

    kv_w_loaded = []

    def p1_kvw0(gpsimd):
        for kv in range(2):
            for nm, dst in [("wkb", wkbf[kv]), ("wvb", wvbf[kv])]:
                src = dram[nm][kv * HID:(kv + 1) * HID, :].rearrange(
                    "(fc f) o -> f fc o", f=CH)
                d = dst[:, :].rearrange("f (fc o) -> f fc o", o=CH)
                sW.inc(gpsimd.dma_start(out=d, in_=src), 16)
            kv_w_loaded.append(sW.n)
    sec("gpsimd", p1_kvw0)

    kvout_scr = []

    for kv in range(KVH):
        bi = kv % 2
        wtar = kv_w_loaded[kv]

        def p1_kmm(tensor, bi=bi, wtar=wtar):
            tensor.wait_ge(sW.h, wtar)
            tensor.wait_ge(sV.h, sV.n)
            tensor.wait_ge(sA.h, sA.n)
            for fc in range(NF):
                for th in range(2):
                    sT.inc(tensor.matmul(
                        pq[:, th * 512:(th + 1) * 512],
                        wkbf[bi][:, fc * CH:(fc + 1) * CH],
                        x1f[:, fc * TOK + th * 512: fc * TOK + (th + 1) * 512],
                        start=(fc == 0), stop=(fc == NF - 1)))
        sec("tensor", p1_kmm)
        kmm_t = sT.n

        def p1_vmm(tensor, bi=bi, wtar=wtar):
            tensor.wait_ge(sW.h, wtar)
            for fc in range(NF):
                for th in range(2):
                    sT.inc(tensor.matmul(
                        pg[:, th * 512:(th + 1) * 512],
                        wvbf[bi][:, fc * CH:(fc + 1) * CH],
                        x1f[:, fc * TOK + th * 512: fc * TOK + (th + 1) * 512],
                        start=(fc == 0), stop=(fc == NF - 1)))
        sec("tensor", p1_vmm)
        vmm_t = sT.n

        def p1_kev(vector, kmm_t=kmm_t):
            vector.wait_ge(sT.h, kmm_t)
            sV.inc(vector.tensor_copy(ktmp[:, :], pq[:, :]))
        sec("vector", p1_kev)
        kev_v = sV.n

        def p1_krot(tensor, kev_v=kev_v):
            tensor.wait_ge(sV.h, kev_v)
            for th in range(2):
                sT.inc(tensor.matmul(pr[:, th * 512:(th + 1) * 512], rot[:, :],
                                     ktmp[:, th * 512:(th + 1) * 512],
                                     start=True, stop=True))
        sec("tensor", p1_krot)
        krot_t = sT.n

        def p1_krope(vector, bi=bi, kv=kv, krot_t=krot_t):
            v = vector
            v.wait_ge(sT.h, krot_t)
            if kv >= 2:
                v.wait_ge(sScr.h, kvout_scr[kv - 2])
            sV.inc(v.tensor_mul(ropeA[:, :], ktmp[:, :], cosfm[:, :]))
            sV.inc(v.tensor_mul(ropeB[:, :], pr[:, :], sin2fm[:, :]))
            sV.inc(v.tensor_add(kfm[bi][:, :], ropeA[:, :], ropeB[:, :]))
        sec("vector", p1_krope)
        krope_v = sV.n

        def p1_ktr(tensor, bi=bi, krope_v=krope_v):
            tensor.wait_ge(sV.h, krope_v)
            tensor.wait_ge(sA.h, sA.n)
            for n in range(NCH):
                sT.inc(tensor.transpose(ps1[:, n * 64:(n + 1) * 64].bitcast(BF16),
                                        kfm[bi][:, n * CH:(n + 1) * CH], id16[:, :]))
        sec("tensor", p1_ktr)
        ktr_t = sT.n

        def p1_kte(scalar, bi=bi, kv=kv, ktr_t=ktr_t):
            scalar.wait_ge(sT.h, ktr_t)
            if kv >= 2:
                scalar.wait_ge(sScr.h, kvout_scr[kv - 2])
            sA.inc(scalar.copy(ktm[bi][:, :], ps1[:, 0:512].bitcast(BF16)))
        sec("scalar", p1_kte)

        def p1_vev(vector, vmm_t=vmm_t):
            vector.wait_ge(sT.h, vmm_t)
            sV.inc(vector.tensor_copy(ktmp[:, :], pg[:, :]))
        sec("vector", p1_vev)
        vev_v = sV.n

        def p1_vtr(tensor, vev_v=vev_v):
            tensor.wait_ge(sV.h, vev_v)
            tensor.wait_ge(sA.h, sA.n)
            for n in range(NCH):
                sT.inc(tensor.transpose(ps1[:, n * 64:(n + 1) * 64].bitcast(BF16),
                                        ktmp[:, n * CH:(n + 1) * CH], id16[:, :]))
        sec("tensor", p1_vtr)
        vtr_t = sT.n

        def p1_vte(scalar, bi=bi, kv=kv, vtr_t=vtr_t):
            scalar.wait_ge(sT.h, vtr_t)
            if kv >= 2:
                scalar.wait_ge(sScr.h, kvout_scr[kv - 2])
            sA.inc(scalar.copy(vtm[bi][:, :], ps1[:, 0:512].bitcast(BF16)))
        sec("scalar", p1_vte)

        kv_ready_a = sA.n

        def p1_kvout(sync, kv=kv, bi=bi, kv_ready_a=kv_ready_a, krope_v=krope_v):
            sync.wait_ge(sA.h, kv_ready_a)
            sync.wait_ge(sV.h, krope_v)
            base = kv * 3 * CH
            sScr.inc(sync.dma_start(out=kvscr[base:base + CH, :], in_=kfm[bi][:, :]), 16)
            sScr.inc(sync.dma_start(out=kvscr[base + CH:base + 2 * CH, :], in_=ktm[bi][:, :]), 16)
            sScr.inc(sync.dma_start(out=kvscr[base + 2 * CH:base + 3 * CH, :], in_=vtm[bi][:, :]), 16)
        sec("sync", p1_kvout)
        kvout_scr.append(sScr.n)

        if kv + 2 < KVH:
            def p1_kvw(gpsimd, kv=kv, bi=bi, vmm_t=vmm_t):
                gpsimd.wait_ge(sT.h, vmm_t)
                for nm, dst in [("wkb", wkbf[bi]), ("wvb", wvbf[bi])]:
                    src = dram[nm][(kv + 2) * HID:(kv + 3) * HID, :].rearrange(
                        "(fc f) o -> f fc o", f=CH)
                    d = dst[:, :].rearrange("f (fc o) -> f fc o", o=CH)
                    sW.inc(gpsimd.dma_start(out=d, in_=src), 16)
                kv_w_loaded.append(sW.n)
            sec("gpsimd", p1_kvw)

        for h in range(4 * kv, 4 * kv + 4):
            hb = h % 2

            def p1_xka(vector, bi=bi, h=h, hb=hb, kv_ready_a=kv_ready_a):
                v = vector
                v.wait_ge(sA.h, kv_ready_a)
                v.wait_ge(sT.h, sT.n)
                i3 = vtm[bi][:, :].rearrange("p (n d) -> p n d", d=CH)
                o3 = xca[hb][:, :].rearrange("p (n d) -> p n d", d=CH)
                sV.inc(v.tensor_tensor(o3, i3, colbcast(dttm, h), ALU.mult))
                i3 = ktm[bi][:, :].rearrange("p (n d) -> p n d", d=CH)
                o3 = kda[hb][:, :].rearrange("p (n d) -> p n d", d=CH)
                sV.inc(v.tensor_tensor(o3, i3, colbcast(ecol, h), ALU.mult))
            sec("vector", p1_xka)
            xk_v = sV.n

            for n in range(NCH):
                col = n * 32 + h

                def p1_cst(tensor, hb=hb, n=n, xk_v=xk_v):
                    tensor.wait_ge(sV.h, sV.n if n > 0 else xk_v)
                    sT.inc(tensor.matmul(ps2[:, 0:CH],
                                         kda[hb][:, n * CH:(n + 1) * CH],
                                         xca[hb][:, n * CH:(n + 1) * CH],
                                         start=True, stop=True))
                sec("tensor", p1_cst)
                cst_t = sT.n

                def p1_bupd(vector, h=h, col=col, cst_t=cst_t):
                    vector.wait_ge(sT.h, cst_t)
                    sV.inc(vector.scalar_tensor_tensor(
                        S[:, h * D:(h + 1) * D], S[:, h * D:(h + 1) * D],
                        cdec[:, col:col + 1], ps2[:, 0:CH], ALU.mult, ALU.add))
                sec("vector", p1_bupd)

    B_done_v = sV.n

    def p1_agout(sync):
        sync.wait_ge(sV.h, B_done_v)
        sScr.inc(sync.dma_start(out=ag_in[:, 0:4096], in_=S[:, :]), 16)
        sScr.inc(sync.dma_start(out=ag_in[:, 4096:4128], in_=abc[:, :]), 16)
    sec("sync", p1_agout)
    ag_dma = sScr.n

    def p1_ag(gpsimd):
        gpsimd.wait_ge(sScr.h, ag_dma)
        sG.inc(gpsimd.collective_compute(
            "AllGather", ALU.bypass,
            replica_groups=[[0, 1, 2, 3], [4, 5, 6, 7]],
            ins=[ag_in], outs=[ag_out]))
    sec("gpsimd", p1_ag)
    ag_g = sG.n

    agtmp = salloc("agtmp", [CH, 4128], F32)
    alpha = salloc("alpha", [CH, 32], F32)

    def p1_szero2(vector):
        vector.wait_ge(sScr.h, ag_dma)
        sV.inc(vector.memset(S[:, :], 0.0))
    sec("vector", p1_szero2)

    for d_ in range(4):
        def p1_agin(sync, d_=d_, ag_g=ag_g):
            sync.wait_ge(sG.h, ag_g)
            sync.wait_ge(sV.h, sV.n)
            sScr.inc(sync.dma_start(out=agtmp[:, :],
                                    in_=ag_out[d_ * CH:(d_ + 1) * CH, :]), 16)
        sec("sync", p1_agin)
        agin_d = sScr.n

        def p1_sin(vector, d_=d_, agin_d=agin_d):
            v = vector
            v.wait_ge(sScr.h, agin_d)
            sV.inc(v.tensor_scalar(agtmp[:, 0:4096], agtmp[:, 0:4096],
                                   mskd[:, d_:d_ + 1], None, ALU.mult))
            sV.inc(v.tensor_scalar(alpha[:, :], agtmp[:, 4096:4128],
                                   mskd[:, d_:d_ + 1], ommd[:, d_:d_ + 1],
                                   ALU.mult, ALU.add))
            v.drain()
            for h in range(H):
                sV.inc(v.scalar_tensor_tensor(
                    S[:, h * D:(h + 1) * D], S[:, h * D:(h + 1) * D],
                    alpha[:, h:h + 1], agtmp[:, h * D:(h + 1) * D],
                    ALU.mult, ALU.add))
        sec("vector", p1_sin)

    SIN_V = sV.n
    P1_T = sT.n
    sfree("alpha", "agtmp", "wvbf1", "wvbf0", "wkbf1", "wkbf0", "winbf",
          "ghm", "dthm", "ktmp")

    wqbf = [salloc(f"wqbf{i}", [CH, HID], BF16) for i in range(2)]
    wgbf = [salloc(f"wgbf{i}", [CH, HID], BF16) for i in range(2)]
    qtmp = salloc("qtmp", [CH, TOK], BF16)
    qr_t = [salloc(f"qr_t{i}", [CH, TOK], BF16) for i in range(2)]
    gs_t = [salloc(f"gs_t{i}", [CH, TOK], BF16) for i in range(2)]
    sbf_t = salloc("sbf_t", [CH, CH], BF16)
    growa = salloc("growa", [1, TOK], F32)
    deca = [salloc(f"deca{i}", [CH, TOK], F32) for i in range(2)]
    ega = salloc("ega", [CH, TOK], BF16)
    qga = [salloc(f"qga{i}", [CH, TOK], BF16) for i in range(2)]
    sd_t = salloc("sd_t", [CH, CH], BF16)
    ygh = [salloc(f"ygh{i}", [CH, TOK], BF16) for i in range(2)]
    ybuf = [salloc(f"ybuf{i}", [CH, TOK], BF16) for i in range(2)]
    ssqa = salloc("ssqa", [CH, 48], F32)
    sq128 = salloc("sq128", [CH, CH], BF16)

    q_w_loaded = []

    def p2_w0(gpsimd):
        for hh in range(2):
            for nm, dst in [("wqb", wqbf[hh]), ("wgb", wgbf[hh])]:
                src = dram[nm][hh * HID:(hh + 1) * HID, :].rearrange(
                    "(fc f) o -> f fc o", f=CH)
                d = dst[:, :].rearrange("f (fc o) -> f fc o", o=CH)
                sW.inc(gpsimd.dma_start(out=d, in_=src), 16)
            q_w_loaded.append(sW.n)
    sec("gpsimd", p2_w0)

    kvbuf_loaded = []

    def p2_kv0(sync):
        sync.wait_ge(sScr.h, ag_dma)
        for kvi in range(2):
            base = kvi * 3 * CH
            sScr.inc(sync.dma_start(out=kfm[kvi][:, :], in_=kvscr[base:base + CH, :]), 16)
            sScr.inc(sync.dma_start(out=ktm[kvi][:, :], in_=kvscr[base + CH:base + 2 * CH, :]), 16)
            sScr.inc(sync.dma_start(out=vtm[kvi][:, :], in_=kvscr[base + 2 * CH:base + 3 * CH, :]), 16)
            kvbuf_loaded.append(sScr.n)
    sec("sync", p2_kv0)

    ygout_scr = []
    supd_all_prev = [0]
    silu_prev = [0]
    ega_xka_prev = [0]
    egact_prev = [0]
    ygh_prev = {}

    def emit_prep(h, part="ab", st=None):
        kv = h // 4
        bi = h % 2
        kvbi = kv % 2
        wtar = q_w_loaded[h]
        kvtar = kvbuf_loaded[kv]
        if st is None:
            st = {}

        if part == "b":
            qev_v = st["_qev_v"]
            return _prep_b(h, st, kv, bi, kvbi, wtar, kvtar, qev_v)

        def p2_qmm(tensor, bi=bi, wtar=wtar, sp=silu_prev[0]):
            t = tensor
            t.wait_ge(sW.h, wtar)
            if sp:
                t.wait_ge(sA.h, sp)
            if h == 0:
                t.wait_ge(sV.h, SIN_V)
            for fc in range(NF):
                for th in range(2):
                    sT.inc(t.matmul(
                        pq[:, th * 512:(th + 1) * 512],
                        wqbf[bi][:, fc * CH:(fc + 1) * CH],
                        x1f[:, fc * TOK + th * 512: fc * TOK + (th + 1) * 512],
                        start=(fc == 0), stop=(fc == NF - 1)))
        sec("tensor", p2_qmm)
        qmm_t = sT.n

        def p2_qev(vector, qmm_t=qmm_t):
            vector.wait_ge(sT.h, qmm_t)
            vector.wait_ge(sT.h, sT.n if False else qmm_t)
            sV.inc(vector.tensor_copy(qtmp[:, :], pq[:, :]))
        sec("vector", p2_qev)
        qev_v = sV.n
        st["_qev_v"] = qev_v
        if part == "a":
            return st
        return _prep_b(h, st, kv, bi, kvbi, wtar, kvtar, qev_v)

    def _prep_b(h, st, kv, bi, kvbi, wtar, kvtar, qev_v):
        def p2_gmm(tensor, bi=bi, qev_v=qev_v):
            t = tensor
            t.wait_ge(sV.h, qev_v)
            for fc in range(NF):
                for th in range(2):
                    sT.inc(t.matmul(
                        pq[:, th * 512:(th + 1) * 512],
                        wgbf[bi][:, fc * CH:(fc + 1) * CH],
                        x1f[:, fc * TOK + th * 512: fc * TOK + (th + 1) * 512],
                        start=(fc == 0), stop=(fc == NF - 1)))
        sec("tensor", p2_gmm)
        gmm_t = sT.n

        def p2_gs(scalar, gmm_t=gmm_t, bi=bi, h=h):
            s = scalar
            s.wait_ge(sT.h, gmm_t)
            if h >= 2:
                s.wait_ge(sV.h, ygh_prev[h - 2])
            sA.inc(s.activation(gs_t[bi][:, :], pq[:, :], ACTF.Silu))
        sec("scalar", p2_gs)
        st["silu_a"] = sA.n
        silu_prev[0] = sA.n

        def p2_qrot(tensor, qev_v=qev_v, ea=egact_prev[0]):
            t = tensor
            t.wait_ge(sV.h, qev_v)
            if ea:
                t.wait_ge(sA.h, ea)
            for th in range(2):
                sT.inc(t.matmul(pr[:, th * 512:(th + 1) * 512], rot[:, :],
                                qtmp[:, th * 512:(th + 1) * 512],
                                start=True, stop=True))
        sec("tensor", p2_qrot)
        qrot_t = sT.n

        def p2_qrope(vector, qrot_t=qrot_t, bi=bi):
            v = vector
            v.wait_ge(sT.h, qrot_t)
            v.wait_ge(sT.h, sT.n)
            sV.inc(v.tensor_mul(ropeA[:, :], qtmp[:, :], cosfm[:, :]))
            sV.inc(v.tensor_mul(ropeB[:, :], pr[:, :], sin2fm[:, :]))
            sV.inc(v.tensor_add(qr_t[bi][:, :], ropeA[:, :], ropeB[:, :]))
        sec("vector", p2_qrope)
        qrope_v = sV.n
        st["qrope_v"] = qrope_v

        if h + 2 < H:
            def p2_wn(gpsimd, h=h, bi=bi, gmm_t=gmm_t):
                gpsimd.wait_ge(sT.h, gmm_t)
                for nm, dst in [("wqb", wqbf[bi]), ("wgb", wgbf[bi])]:
                    src = dram[nm][(h + 2) * HID:(h + 3) * HID, :].rearrange(
                        "(fc f) o -> f fc o", f=CH)
                    d = dst[:, :].rearrange("f (fc o) -> f fc o", o=CH)
                    sW.inc(gpsimd.dma_start(out=d, in_=src), 16)
                q_w_loaded.append(sW.n)
            sec("gpsimd", p2_wn)

        def p2_grows(tensor, h=h, qrope_v=qrope_v):
            t = tensor
            t.wait_ge(sV.h, qrope_v)
            for n in range(NCH):
                c0 = (n % 4) * CH
                ps = pr[0:1, 0:512] if n < 4 else pr[0:1, 512:1024]
                sT.inc(t.matmul(ps[0:1, c0:c0 + CH], gtm[:, n * 32 + h:n * 32 + h + 1],
                                tri[:, :], start=True, stop=True))
        sec("tensor", p2_grows)
        grows_t = sT.n

        def p2_growev(scalar, grows_t=grows_t):
            s = scalar
            s.wait_ge(sT.h, grows_t)
            sA.inc(s.copy(growa[0:1, :], pr[0:1, :]))
        sec("scalar", p2_growev)
        growev_a = sA.n

        def p2_m1mm(tensor, growev_a=growev_a):
            t = tensor
            t.wait_ge(sA.h, growev_a)
            for n in range(NCH):
                sT.inc(t.matmul(pr[:, n * CH:(n + 1) * CH], ones1[0:1, :],
                                growa[0:1, n * CH:(n + 1) * CH], start=True, stop=False))
                sT.inc(t.matmul(pr[:, n * CH:(n + 1) * CH],
                                growa[0:1, n * CH:(n + 1) * CH], negones1[0:1, :],
                                start=False, stop=True))
        sec("tensor", p2_m1mm)
        m1mm_t = sT.n

        def p2_m1add(vector, m1mm_t=m1mm_t, bi=bi):
            vector.wait_ge(sT.h, m1mm_t)
            p3 = pr[:, :].rearrange("p (n i) -> p n i", i=CH)
            mk = maskneg[:, :].unsqueeze(1).broadcast_to([CH, NCH, CH])
            o3 = deca[bi][:, :].rearrange("p (n i) -> p n i", i=CH)
            sV.inc(vector.tensor_tensor(o3, p3, mk, ALU.add))
        sec("vector", p2_m1add)
        m1add_v = sV.n

        def p2_dec(scalar, m1add_v=m1add_v, bi=bi):
            scalar.wait_ge(sV.h, m1add_v)
            sA.inc(scalar.activation(deca[bi][:, :], deca[bi][:, :], ACTF.Exp))
        sec("scalar", p2_dec)
        st["dec_a"] = sA.n

        def p2_egmm(tensor, m1add_v=m1add_v):
            t = tensor
            t.wait_ge(sV.h, m1add_v)
            for n in range(NCH):
                sT.inc(t.matmul(pr[:, n * CH:(n + 1) * CH], ones1[0:1, :],
                                growa[0:1, n * CH:(n + 1) * CH], start=True, stop=True))
        sec("tensor", p2_egmm)
        egmm_t = sT.n

        def p2_eg(scalar, egmm_t=egmm_t, ea=ega_xka_prev[0]):
            s = scalar
            s.wait_ge(sT.h, egmm_t)
            if ea:
                s.wait_ge(sV.h, ea)
            sA.inc(s.activation(ega[:, :], pr[:, :], ACTF.Exp))
        sec("scalar", p2_eg)
        eg_a = sA.n
        egact_prev[0] = eg_a

        def p2_xka(vector, h=h, kvbi=kvbi, bi=bi, kvtar=kvtar, eg_a=eg_a):
            v = vector
            v.wait_ge(sScr.h, kvtar)
            v.wait_ge(sT.h, sT.n)
            v.wait_ge(sA.h, eg_a)
            i3 = vtm[kvbi][:, :].rearrange("p (n d) -> p n d", d=CH)
            o3 = xca[bi][:, :].rearrange("p (n d) -> p n d", d=CH)
            sV.inc(v.tensor_tensor(o3, i3, colbcast(dttm, h), ALU.mult))
            i3 = ktm[kvbi][:, :].rearrange("p (n d) -> p n d", d=CH)
            o3 = kda[bi][:, :].rearrange("p (n d) -> p n d", d=CH)
            sV.inc(v.tensor_tensor(o3, i3, colbcast(ecol, h), ALU.mult))
            sV.inc(v.tensor_mul(qga[bi][:, :], qr_t[bi][:, :], ega[:, :]))
        sec("vector", p2_xka)
        st["xka_v"] = sV.n
        ega_xka_prev[0] = sV.n
        return st

    def emit_nloop(h, st, n0=0, n1=NCH):
        kv = h // 4
        bi = h % 2
        kvbi = kv % 2
        if n0 == 0:
            st["_sd_vs"] = []
            st["_supd_prev"] = [0]
            st["_yev_prev"] = [0]
            st["_sqa_prev"] = [0]
        sd_vs = st["_sd_vs"]
        supd_prev = st["_supd_prev"]
        yev_prev = st["_yev_prev"]
        sqa_prev = st["_sqa_prev"]
        if n0 == 0:
            def p2_cstb(tensor, h=h, bi=bi, st=st):
                t = tensor
                t.wait_ge(sV.h, st["xka_v"])
                t.wait_ge(sV.h, supd_all_prev[0])
                for n in range(NCH):
                    sT.inc(t.matmul(pg[:, n * CH:(n + 1) * CH],
                                    kda[bi][:, n * CH:(n + 1) * CH],
                                    xca[bi][:, n * CH:(n + 1) * CH],
                                    start=True, stop=True))
            sec("tensor", p2_cstb)
            st["_cstb_t"] = sT.n
        cstb_t = st["_cstb_t"]
        for n in range(n0, n1):
            col = n * 32 + h

            def p2_sbf(vector, h=h):
                sV.inc(vector.tensor_copy(sbf_t[:, :], S[:, h * D:(h + 1) * D]))
            sec("vector", p2_sbf)
            sbf_v = sV.n

            def p2_mm1(tensor, n=n, h=h, bi=bi, kvbi=kvbi, st=st):
                t = tensor
                if n == 0:
                    t.wait_ge(sV.h, st["xka_v"])
                    t.wait_ge(sV.h, ygh_prev.get(h - 1, 0))
                else:
                    t.wait_ge(sV.h, sd_vs[n - 1])
                sT.inc(t.matmul(ps2[:, 256:384], kfm[kvbi][:, n * CH:(n + 1) * CH],
                                qr_t[bi][:, n * CH:(n + 1) * CH], start=True, stop=True))
            sec("tensor", p2_mm1)
            mm1_t = sT.n

            def p2_sd(vector, n=n, mm1_t=mm1_t, st=st, bi=bi):
                v = vector
                v.wait_ge(sT.h, mm1_t)
                if n == 0:
                    v.wait_ge(sA.h, st["dec_a"])
                sV.inc(v.tensor_mul(sd_t[:, :], ps2[:, 256:384],
                                    deca[bi][:, n * CH:(n + 1) * CH]))
            sec("vector", p2_sd)
            sd_v = sV.n
            sd_vs.append(sd_v)

            def p2_ymm(tensor, n=n, bi=bi, sd_v=sd_v):
                t = tensor
                t.wait_ge(sV.h, sd_v)
                if yev_prev[0]:
                    t.wait_ge(sV.h, yev_prev[0])
                if sqa_prev[0]:
                    t.wait_ge(sA.h, sqa_prev[0])
                sT.inc(t.matmul(ps1[:, 0:CH], sd_t[:, :],
                                xca[bi][:, n * CH:(n + 1) * CH], start=True, stop=False))
                sT.inc(t.matmul(ps1[:, 0:CH], qga[bi][:, n * CH:(n + 1) * CH],
                                sbf_t[:, :], start=False, stop=True))
            sec("tensor", p2_ymm)
            ymm_t = sT.n

            def p2_supd(vector, h=h, n=n, col=col, cstb_t=cstb_t):
                vector.wait_ge(sT.h, cstb_t)
                sV.inc(vector.scalar_tensor_tensor(
                    S[:, h * D:(h + 1) * D], S[:, h * D:(h + 1) * D],
                    cdec[:, col:col + 1], pg[:, n * CH:(n + 1) * CH],
                    ALU.mult, ALU.add))
            sec("vector", p2_supd)
            supd_prev[0] = sV.n
            if n == NCH - 1:
                supd_all_prev[0] = sV.n

            def p2_sqa(scalar, n=n, bi=bi, ymm_t=ymm_t):
                s = scalar
                s.wait_ge(sT.h, ymm_t)
                sA.inc(s.activation(sq128[:, :], ps1[:, 0:CH], ACTF.Square,
                                    accum_out=ssqa[:, bi * 24 + n:bi * 24 + n + 1]))
            sec("scalar", p2_sqa)
            sqa_prev[0] = sA.n

            def p2_yev(vector, n=n, bi=bi, ymm_t=ymm_t):
                vector.wait_ge(sT.h, ymm_t)
                sV.inc(vector.tensor_copy(ybuf[bi][:, n * CH:(n + 1) * CH],
                                          ps1[:, 0:CH]))
            sec("vector", p2_yev)
            yev_prev[0] = sV.n

        st["loop_v"] = sV.n
        st["loop_a"] = sA.n
        if n1 < NCH:
            return

        if h % 4 == 3 and kv + 2 < KVH:
            def p2_kvn(sync, kv=kv, kvbi=kvbi):
                sync.wait_ge(sT.h, sT.n)
                sync.wait_ge(sA.h, sA.n)
                sync.wait_ge(sV.h, sV.n)
                base = (kv + 2) * 3 * CH
                sScr.inc(sync.dma_start(out=kfm[kvbi][:, :], in_=kvscr[base:base + CH, :]), 16)
                sScr.inc(sync.dma_start(out=ktm[kvbi][:, :], in_=kvscr[base + CH:base + 2 * CH, :]), 16)
                sScr.inc(sync.dma_start(out=vtm[kvbi][:, :], in_=kvscr[base + 2 * CH:base + 3 * CH, :]), 16)
                kvbuf_loaded.append(sScr.n)
            sec("sync", p2_kvn)

    def emit_tail(h, st):
        bi = h % 2

        def p2_rms8(vector, st=st, bi=bi):
            v = vector
            v.wait_ge(sA.h, st["loop_a"])
            v.wait_ge(sV.h, st["loop_v"])
            v.drain()
            sV.inc(v.tensor_scalar(ssqa[:, bi * 24 + 8:bi * 24 + 16],
                                   ssqa[:, bi * 24:bi * 24 + 8],
                                   1.0 / 128.0, 1e-5, ALU.mult, ALU.add))
            v.drain()
        sec("vector", p2_rms8)
        rms8_v = sV.n

        def p2_rms8a(scalar, rms8_v=rms8_v, bi=bi):
            s = scalar
            s.wait_ge(sV.h, rms8_v)
            sA.inc(s.activation(ssqa[:, bi * 24 + 16:bi * 24 + 24],
                                ssqa[:, bi * 24 + 8:bi * 24 + 16], ACTF.Ln))
            s.drain()
            sA.inc(s.activation(ssqa[:, bi * 24 + 8:bi * 24 + 16],
                                ssqa[:, bi * 24 + 16:bi * 24 + 24], ACTF.Exp, scale=-0.5))
        sec("scalar", p2_rms8a)
        rms8_a = sA.n

        def p2_yna(vector, bi=bi, rms8_a=rms8_a):
            v = vector
            v.wait_ge(sA.h, rms8_a)
            i3 = ybuf[bi][:, :].rearrange("p (n d) -> p n d", d=CH)
            rmsb = ssqa[:, bi * 24 + 8:bi * 24 + 16].unsqueeze(2).broadcast_to(
                [CH, NCH, CH])
            sV.inc(v.tensor_tensor(i3, i3, rmsb, ALU.mult))
        sec("vector", p2_yna)
        yna_v = sV.n

        yg_prev = [0]
        for n in range(NCH):
            def p2_ytr(tensor, n=n, bi=bi, yna_v=yna_v):
                t = tensor
                t.wait_ge(sV.h, yna_v)
                if yg_prev[0]:
                    t.wait_ge(sV.h, yg_prev[0])
                sT.inc(t.transpose(ps2[:, 0:64].bitcast(BF16),
                                   ybuf[bi][:, n * CH:(n + 1) * CH], id16[:, :]))
            sec("tensor", p2_ytr)
            ytr_t = sT.n

            def p2_yg(vector, h=h, n=n, bi=bi, ytr_t=ytr_t):
                v = vector
                v.wait_ge(sT.h, ytr_t)
                if n == 0 and h >= 2:
                    v.wait_ge(sScr.h, ygout_scr[h - 2])
                sV.inc(v.scalar_tensor_tensor(
                    ygh[bi][:, n * CH:(n + 1) * CH], ps2[:, 0:64].bitcast(BF16),
                    gnormc[:, 0:1],
                    gs_t[bi][:, n * CH:(n + 1) * CH], ALU.mult, ALU.mult))
            sec("vector", p2_yg)
            yg_prev[0] = sV.n

        ygh_prev[h] = sV.n

        def p2_ygout(sync, h=h, bi=bi, t=sV.n):
            sync.wait_ge(sV.h, t)
            sScr.inc(sync.dma_start(out=ygscr[h * CH:(h + 1) * CH, :], in_=ygh[bi][:, :]), 16)
        sec("sync", p2_ygout)
        ygout_scr.append(sScr.n)

    prep_st = [None] * H
    prep_st = [None] * H
    prep_st[0] = emit_prep(0)
    for h in range(H):
        if h >= 1:
            emit_tail(h - 1, prep_st[h - 1])
        if h + 1 < H:
            prep_st[h + 1] = emit_prep(h + 1, part="a")
        emit_nloop(h, prep_st[h], 0, 4)
        if h + 1 < H:
            emit_prep(h + 1, part="b", st=prep_st[h + 1])
        emit_nloop(h, prep_st[h], 4, NCH)
    emit_tail(H - 1, prep_st[H - 1])

    P2_SCR = sScr.n
    P2_V = sV.n
    P2_A = sA.n
    P2_T = sT.n
    sfree("sq128", "ssqa", "ybuf1", "ybuf0", "ygh1", "ygh0", "sd_t",
          "qga1", "qga0", "ega", "deca1", "deca0", "growa", "sbf_t",
          "gs_t1", "gs_t0", "qr_t1", "qr_t0", "qtmp",
          "wgbf1", "wgbf0", "wqbf1", "wqbf0",
          "kda1", "kda0", "xca1", "xca0", "vtm1", "vtm0", "ktm1", "ktm0",
          "kfm1", "kfm0", "ropeB", "ropeA", "S")

    ygf = x1f
    xmid = salloc("xmid", [CH, NCH * HID], BF16)
    wobf = [salloc(f"wobf{i}", [CH, HID], BF16) for i in range(2)]
    xcb = [salloc(f"xcb{i}", [CH, TOK], F32) for i in range(2)]
    at_t = salloc("at_t", [CH, TOK], BF16)
    tstat = salloc("tstat", [CH, TOK], F32)

    def p3_ygin(sync):
        sync.wait_ge(sScr.h, P2_SCR)
        sync.wait_ge(sT.h, P2_T)
        for h in range(H):
            sScr.inc(sync.dma_start(out=ygf[:, h * TOK:(h + 1) * TOK],
                                    in_=ygscr[h * CH:(h + 1) * CH, :]), 16)
    sec("sync", p3_ygin)
    ygin_d = sScr.n

    wo_loaded = []

    def p3_w0(gpsimd):
        gpsimd.wait_ge(sT.h, P2_T)
        for ob in range(2):
            src = dram["wob"][ob * HID:(ob + 1) * HID, :].rearrange(
                "(fc f) o -> f fc o", f=CH)
            d = wobf[ob][:, :].rearrange("f (fc o) -> f fc o", o=CH)
            sW.inc(gpsimd.dma_start(out=d, in_=src), 16)
            wo_loaded.append(sW.n)
    sec("gpsimd", p3_w0)

    x_cb_loaded = []

    def p3_x0(sync):
        sync.wait_ge(sT.h, P2_T)
        for ob in range(2):
            src = dram["x"][:, :].rearrange("(tc t) (obx f) -> obx t tc f",
                                            t=CH, f=CH)[ob]
            d = xcb[ob][:, :].rearrange("t (tc f) -> t tc f", f=CH)
            sIn.inc(sync.dma_start(out=d, in_=src), 16)
            x_cb_loaded.append(sIn.n)
    sec("sync", p3_x0)

    ev_prev = [0]
    add_prev = [0]
    for ob in range(32):
        bi = ob % 2
        wtar = wo_loaded[ob]
        xtar = x_cb_loaded[ob]

        def p3_mm(tensor, ob=ob, bi=bi, wtar=wtar):
            t = tensor
            t.wait_ge(sW.h, wtar)
            t.wait_ge(sV.h, ev_prev[0])
            if ob == 0:
                t.wait_ge(sScr.h, ygin_d)
            for fc in range(NF):
                for th in range(2):
                    sT.inc(t.matmul(
                        pq[:, th * 512:(th + 1) * 512],
                        wobf[bi][:, fc * CH:(fc + 1) * CH],
                        ygf[:, fc * TOK + th * 512: fc * TOK + (th + 1) * 512],
                        start=(fc == 0), stop=(fc == NF - 1)))
        sec("tensor", p3_mm)
        mm_t = sT.n

        def p3_ev(vector, mm_t=mm_t):
            vector.wait_ge(sT.h, mm_t)
            sV.inc(vector.tensor_copy(at_t[:, :], pq[:, :]))
        sec("vector", p3_ev)
        ev_v = sV.n
        ev_prev[0] = ev_v

        if ob + 2 < 32:
            def p3_wn(gpsimd, ob=ob, bi=bi, mm_t=mm_t):
                gpsimd.wait_ge(sT.h, mm_t)
                src = dram["wob"][(ob + 2) * HID:(ob + 3) * HID, :].rearrange(
                    "(fc f) o -> f fc o", f=CH)
                d = wobf[bi][:, :].rearrange("f (fc o) -> f fc o", o=CH)
                sW.inc(gpsimd.dma_start(out=d, in_=src), 16)
                wo_loaded.append(sW.n)
            sec("gpsimd", p3_wn)

        def p3_tr(tensor, ev_v=ev_v):
            t = tensor
            t.wait_ge(sV.h, ev_v)
            t.wait_ge(sV.h, add_prev[0])
            for tc in range(NCH):
                sT.inc(t.transpose(pr[:, tc * 64:(tc + 1) * 64].bitcast(BF16),
                                   at_t[:, tc * CH:(tc + 1) * CH], id16[:, :]))
        sec("tensor", p3_tr)
        tr_t = sT.n

        def p3_add(vector, ob=ob, bi=bi, tr_t=tr_t, xtar=xtar):
            v = vector
            v.wait_ge(sT.h, tr_t)
            v.wait_ge(sIn.h, xtar)
            o3 = xmid[:, :].rearrange("p (tc f) -> p tc f", f=HID)[
                :, :, ob * CH:(ob + 1) * CH]
            i0 = pr[:, 0:512].bitcast(BF16).rearrange("p (tc d) -> p tc d", d=CH)
            i1 = xcb[bi][:, :].rearrange("p (tc d) -> p tc d", d=CH)
            sV.inc(v.tensor_add(o3, i0, i1))
        sec("vector", p3_add)
        add_prev[0] = sV.n

        if ob + 2 < 32:
            def p3_xn(sync, ob=ob, bi=bi):
                sync.wait_ge(sV.h, add_prev[0])
                src = dram["x"][:, :].rearrange("(tc t) (obx f) -> obx t tc f",
                                                t=CH, f=CH)[ob + 2]
                d = xcb[bi][:, :].rearrange("t (tc f) -> t tc f", f=CH)
                sIn.inc(sync.dma_start(out=d, in_=src), 16)
                x_cb_loaded.append(sIn.n)
            sec("sync", p3_xn)

    XMID_V = sV.n

    p3b_r_v = []
    for tc in range(NCH):
        so = 24 * (tc % 2)

        def p3b_s1(vector, tc=tc, so=so):
            v = vector
            if tc == 0:
                v.wait_ge(sV.h, XMID_V)
            sV.inc(v.reduce_sum(stats[:, so + 0:so + 1],
                                xmid[:, tc * HID:(tc + 1) * HID], axis=AX.X))
        sec("vector", p3b_s1)

        def p3b_sq(scalar, tc=tc, so=so):
            if tc == 0:
                scalar.wait_ge(sV.h, XMID_V)
            if tc >= 2:
                scalar.wait_ge(sV.h, p3b_r_v[tc - 2])
            for q4 in range(4):
                sA.inc(scalar.activation(pq[:, :],
                                         xmid[:, tc * HID + q4 * TOK: tc * HID + (q4 + 1) * TOK],
                                         ACTF.Square,
                                         accum_out=stats[:, so + 12 + q4:so + 13 + q4]))
        sec("scalar", p3b_sq)
        sq_a = sA.n

        def p3b_chain(vector, sq_a=sq_a, so=so):
            v = vector
            v.wait_ge(sA.h, sq_a)
            sV.inc(v.tensor_add(stats[:, so + 1:so + 2], stats[:, so + 12:so + 13], stats[:, so + 13:so + 14]))
            v.drain()
            sV.inc(v.tensor_add(stats[:, so + 16:so + 17], stats[:, so + 14:so + 15], stats[:, so + 15:so + 16]))
            v.drain()
            sV.inc(v.tensor_add(stats[:, so + 1:so + 2], stats[:, so + 1:so + 2], stats[:, so + 16:so + 17]))
            v.drain()
            sV.inc(v.tensor_scalar_mul(stats[:, so + 2:so + 3], stats[:, so + 0:so + 1], 1.0 / HID))
            v.drain()
            sV.inc(v.tensor_scalar_mul(stats[:, so + 3:so + 4], stats[:, so + 1:so + 2], 1.0 / HID))
            v.drain()
            sV.inc(v.tensor_mul(stats[:, so + 4:so + 5], stats[:, so + 2:so + 3], stats[:, so + 2:so + 3]))
            v.drain()
            sV.inc(v.tensor_sub(stats[:, so + 5:so + 6], stats[:, so + 3:so + 4], stats[:, so + 4:so + 5]))
            v.drain()
            sV.inc(v.tensor_scalar(stats[:, so + 5:so + 6], stats[:, so + 5:so + 6], 1.0, 1e-5,
                                   ALU.mult, ALU.add))
        sec("vector", p3b_chain)
        ch_v = sV.n

        def p3b_sqrt(scalar, ch_v=ch_v, so=so):
            scalar.wait_ge(sV.h, ch_v)
            sA.inc(scalar.activation(stats[:, so + 6:so + 7], stats[:, so + 5:so + 6], ACTF.Sqrt))
        sec("scalar", p3b_sqrt)
        sq2_a = sA.n

        def p3b_r(vector, tc=tc, sq2_a=sq2_a, so=so):
            v = vector
            v.wait_ge(sA.h, sq2_a)
            sV.inc(v.reciprocal(stats[:, so + 7:so + 8], stats[:, so + 6:so + 7]))
            v.drain()
            sV.inc(v.tensor_mul(stats[:, so + 8:so + 9], stats[:, so + 7:so + 8], stats[:, so + 7:so + 8]))
            v.drain()
            sV.inc(v.tensor_mul(stats[:, so + 8:so + 9], stats[:, so + 8:so + 9], stats[:, so + 5:so + 6]))
            v.drain()
            sV.inc(v.tensor_scalar(stats[:, so + 8:so + 9], stats[:, so + 8:so + 9], -0.5, 1.5,
                                   ALU.mult, ALU.add))
            v.drain()
            sV.inc(v.tensor_mul(r2c[:, tc:tc + 1], stats[:, so + 7:so + 8], stats[:, so + 8:so + 9]))
            v.drain()
            sV.inc(v.tensor_copy(m2c[:, tc:tc + 1], stats[:, so + 2:so + 3]))
            v.drain()
        sec("vector", p3b_r)
        p3b_r_v.append(sV.n)

    for tc in range(NCH):
        so = 24 * (tc % 2)

        def p3c_am(vector, tc=tc, so=so):
            v = vector
            for q4 in range(4):
                sV.inc(v.tensor_scalar(tstat[:, :],
                                       xmid[:, tc * HID + q4 * TOK: tc * HID + (q4 + 1) * TOK],
                                       m2c[:, tc:tc + 1], None, ALU.subtract))
                sV.inc(v.reduce_max(stats[:, 56:57], tstat[:, :], axis=AX.X,
                                    apply_absolute_value=True))
                v.drain()
                sV.inc(v.tensor_mul(stats[:, 57:58], stats[:, 56:57], r2c[:, tc:tc + 1]))
                v.drain()
                if tc == 0 and q4 == 0:
                    sV.inc(v.tensor_copy(stats[:, 58:59], stats[:, 57:58]))
                else:
                    sV.inc(v.tensor_max(stats[:, 58:59], stats[:, 58:59], stats[:, 57:58]))
                v.drain()
        sec("vector", p3c_am)
    amcol_v = sV.n

    def p3b_amtr(tensor):
        tensor.wait_ge(sV.h, amcol_v)
        sT.inc(tensor.transpose(ps1[0:1, 0:CH], stats[:, 58:59], id32[:, :]))
    sec("tensor", p3b_amtr)
    amtr_t = sT.n

    def p3b_amfin(vector, amtr_t=amtr_t):
        vector.wait_ge(sT.h, amtr_t)
        sV.inc(vector.reduce_max(amaxsb[0:1, 0:1], ps1[0:1, 0:CH], axis=AX.X))
    sec("vector", p3b_amfin)
    am_v = sV.n

    def p3b_amout(sync):
        sync.wait_ge(sV.h, am_v)
        sScr.inc(sync.dma_start(out=am_in, in_=amaxsb[0:1, 0:2]), 16)
    sec("sync", p3b_amout)
    am_dma = sScr.n

    def p3b_ar(gpsimd):
        gpsimd.wait_ge(sScr.h, am_dma)
        sG.inc(gpsimd.collective_compute(
            "AllReduce", ALU.max,
            replica_groups=[[0, 1, 2, 3, 4, 5, 6, 7]],
            ins=[am_in], outs=[am_out]))
    sec("gpsimd", p3b_ar)
    ar_g = sG.n

    def p3b_amin(sync):
        sync.wait_ge(sG.h, ar_g)
        sScr.inc(sync.dma_start(out=amaxsb[0:1, 2:4], in_=am_out), 16)
    sec("sync", p3b_amin)
    amin_d = sScr.n

    def p3b_scale(vector, amin_d=amin_d):
        v = vector
        v.wait_ge(sScr.h, amin_d)
        sV.inc(v.tensor_scalar(amaxsb[0:1, 2:4], amaxsb[0:1, 2:4], 1e-12, None, ALU.max))
        v.drain()
        sV.inc(v.reciprocal(sclrow[0:1, 0:2], amaxsb[0:1, 2:4]))
        v.drain()
        sV.inc(v.tensor_scalar_mul(sclrow[0:1, 0:2], sclrow[0:1, 0:2], 224.0))
        sV.inc(v.tensor_mul(sclrow[0:1, 2:3], amaxsb[0:1, 2:3], amaxsb[0:1, 3:4]))
        v.drain()
        sV.inc(v.tensor_scalar_mul(sclrow[0:1, 2:3], sclrow[0:1, 2:3], 1.0 / 50176.0))
        v.drain()
    sec("vector", p3b_scale)
    scl_v = sV.n

    def p3b_sbc(tensor, scl_v=scl_v):
        tensor.wait_ge(sV.h, scl_v)
        sT.inc(tensor.matmul(ps1[:, 0:4], ones1[0:1, :], sclrow[0:1, 0:4],
                             start=True, stop=True))
    sec("tensor", p3b_sbc)
    sbc_t = sT.n

    def p3b_sbc2(vector, sbc_t=sbc_t):
        v = vector
        v.wait_ge(sT.h, sbc_t)
        sV.inc(v.tensor_copy(sclbc[:, :], ps1[:, 0:4]))
        v.drain()
        sV.inc(v.tensor_scalar(sc2[:, :], r2c[:, :], sclbc[:, 0:1], None, ALU.mult))
        v.drain()
        sV.inc(v.tensor_mul(nb2[:, :], m2c[:, :], sc2[:, :]))
        v.drain()
        sV.inc(v.tensor_scalar_mul(nb2[:, :], nb2[:, :], -1.0))
        v.drain()
    sec("vector", p3b_sbc2)
    SCL_V = sV.n
    P3_T = sT.n

    sfree("tstat", "at_t", "xcb1", "xcb0", "wobf1", "wobf0")

    x2tmp = salloc("x2tmp", [CH, HID], FP8)
    x2f = x1f[:, 0:NF * TOK // 2].bitcast(FP8)
    wq8 = x1f[:, NF * TOK // 2:NF * TOK].bitcast(FP8)
    fwtmp = [salloc(f"fwtmp{i}", [CH, 1024], F32) for i in range(2)]
    f1_t = salloc("f1_t", [CH, TOK], BF16)
    otA = salloc("otA", [CH, TOK], F32)

    for tc in range(NCH):
        def p5_x2(scalar, tc=tc):
            s = scalar
            if tc == 0:
                s.wait_ge(sV.h, SCL_V)
                s.wait_ge(sT.h, P3_T)
            else:
                s.wait_ge(sV.h, sV.n)
            sA.inc(s.activation(x2tmp[:, :], xmid[:, tc * HID:(tc + 1) * HID],
                                ACTF.Identity, bias=nb2[:, tc:tc + 1], scale=sc2[:, tc:tc + 1]))
        sec("scalar", p5_x2)
        x2_a = sA.n

        for g4 in range(8):
            def p5_tr(tensor, g4=g4, x2_a=x2_a):
                t = tensor
                t.wait_ge(sA.h, x2_a)
                t.wait_ge(sV.h, sV.n)
                for i in range(4):
                    fc = g4 * 4 + i
                    sT.inc(t.transpose(
                        ps2[:, i * 64:(i + 1) * 64].bitcast(FP8)[:, 0:256:2],
                        x2tmp[:, fc * CH:(fc + 1) * CH], id8[:, :]))
            sec("tensor", p5_tr)
            tr_t = sT.n

            def p5_te(vector, tc=tc, g4=g4, tr_t=tr_t):
                vector.wait_ge(sT.h, tr_t)
                dst = x2f.rearrange("p (fc t) -> p fc t", t=TOK)[
                    :, g4 * 4:(g4 + 1) * 4, tc * CH:(tc + 1) * CH]
                src = ps2[:, 0:256].bitcast(FP8)[:, 0:1024:2].rearrange(
                    "p (fc t) -> p fc t", t=CH)
                sV.inc(vector.tensor_copy(dst, src))
            sec("vector", p5_te)

    X2_V = sV.n

    for quarter in range(4):
        fw_loaded = []

        def p5_fw0(gpsimd, quarter=quarter):
            gpsimd.wait_ge(sV.h, X2_V if quarter == 0 else sV.n)
            for fc in range(2):
                sW.inc(gpsimd.dma_start(
                    out=fwtmp[fc][:, :],
                    in_=dram["fwT"][fc * CH:(fc + 1) * CH,
                                    quarter * 1024:(quarter + 1) * 1024]), 16)
                fw_loaded.append(sW.n)
        sec("gpsimd", p5_fw0)

        for fc in range(NF):
            bi = fc % 2
            wtar = fw_loaded[fc]

            def p5_q(vector, fc=fc, bi=bi, wtar=wtar):
                vector.wait_ge(sW.h, wtar)
                sV.inc(vector.tensor_scalar(wq8[:, fc * 1024:(fc + 1) * 1024],
                                            fwtmp[bi][:, :], sclbc[:, 1:2], None,
                                            ALU.mult))
            sec("vector", p5_q)
            q_v = sV.n

            if fc + 2 < NF:
                def p5_fwn(gpsimd, fc=fc, bi=bi, quarter=quarter, q_v=q_v):
                    gpsimd.wait_ge(sV.h, q_v)
                    sW.inc(gpsimd.dma_start(
                        out=fwtmp[bi][:, :],
                        in_=dram["fwT"][(fc + 2) * CH:(fc + 3) * CH,
                                        quarter * 1024:(quarter + 1) * 1024]), 16)
                    fw_loaded.append(sW.n)
                sec("gpsimd", p5_fwn)

        WQ_V = sV.n

        f1_prev = [0]
        oadd_prev = [0]
        for obq in range(8):
            ob = quarter * 8 + obq

            def p5_mm(tensor, obq=obq, WQ_V=WQ_V):
                t = tensor
                t.wait_ge(sV.h, WQ_V if obq == 0 else f1_prev[0])
                for fc in range(NF):
                    for th in range(2):
                        sT.inc(t.matmul(
                            pq[:, th * 512:(th + 1) * 512],
                            wq8[:, fc * 1024 + obq * CH: fc * 1024 + (obq + 1) * CH],
                            x2f[:, fc * TOK + th * 512: fc * TOK + (th + 1) * 512],
                            start=(fc == 0), stop=(fc == NF - 1)))
            sec("tensor", p5_mm)
            mm_t = sT.n

            def p5_f1(vector, ob=ob, mm_t=mm_t):
                vector.wait_ge(sT.h, mm_t)
                sV.inc(vector.tensor_scalar(f1_t[:, :], pq[:, :], sclbc[:, 2:3],
                                            fbias[:, ob:ob + 1], ALU.mult, ALU.add))
            sec("vector", p5_f1)
            f1_v = sV.n
            f1_prev[0] = f1_v

            def p5_otr(tensor, f1_v=f1_v):
                t = tensor
                t.wait_ge(sV.h, f1_v)
                t.wait_ge(sV.h, oadd_prev[0])
                for tc in range(NCH):
                    sT.inc(t.transpose(pr[:, tc * 64:(tc + 1) * 64].bitcast(BF16),
                                       f1_t[:, tc * CH:(tc + 1) * CH], id16[:, :]))
            sec("tensor", p5_otr)
            otr_t = sT.n

            def p5_oadd(vector, ob=ob, otr_t=otr_t):
                v = vector
                v.wait_ge(sT.h, otr_t)
                if sOut.n >= 128:
                    v.wait_ge(sOut.h, sOut.n - 128)
                o3 = otA[:, :].rearrange("p (tc d) -> p tc d", d=CH)
                i0 = pr[:, 0:512].bitcast(BF16).rearrange("p (tc d) -> p tc d", d=CH)
                i1 = xmid[:, :].rearrange("p (tc f) -> p tc f", f=HID)[
                    :, :, ob * CH:(ob + 1) * CH]
                sV.inc(v.tensor_add(o3, i0, i1))
            sec("vector", p5_oadd)
            oadd_v = sV.n
            oadd_prev[0] = oadd_v

            def p5_out(sync, ob=ob, oadd_v=oadd_v):
                sync.wait_ge(sV.h, oadd_v)
                for tc in range(NCH):
                    sOut.inc(sync.dma_start(
                        out=y_out[tc * CH:(tc + 1) * CH, ob * CH:(ob + 1) * CH],
                        in_=otA[:, tc * CH:(tc + 1) * CH]), 16)
            sec("sync", p5_out)

    def fin(sync):
        sync.wait_ge(sOut.h, sOut.n)
    sec("sync", fin)

    block_cm.__exit__(None, None, None)
    for nm in reversed(list(LIVE)):
        LIVE.pop(nm).__exit__(None, None, None)
    for nm in reversed(list(pcm)):
        pcm.pop(nm).__exit__(None, None, None)
    return nc


_NC_CACHE = None


def _consts():
    j = np.arange(CH)
    tri = (j[:, None] <= j[None, :]).astype(np.float32)
    ones128 = np.ones((CH, CH), np.float32)
    ones1 = np.ones((1, CH), np.float32)
    rotm = np.zeros((CH, CH), np.float32)
    for d in range(CH):
        rotm[(d + 64) % CH, d] = 1.0
    maskneg = np.where(j[:, None] <= j[None, :], 0.0, -1e9).astype(np.float32)
    ident = np.eye(CH, dtype=np.float32)
    return dict(
        tri=tri, ones128=ones128, ones1=ones1,
        rot=rotm.astype(NPBF), id16=ident.astype(NPBF), id32=ident,
        id8=ident.astype(ml_dtypes.float8_e4m3), maskneg=maskneg,
        negones1=-ones1)


def kernel(L0_attn_hidden_states, L0_attn_Wq, L0_attn_Wk, L0_attn_Wv, L0_attn_Wo,
           L0_attn_Wg, L0_attn_Win, L0_attn_dt_bias, L0_attn_A_log_bias,
           L0_attn_gnorm_w, L0_ffn_weight, L0_ffn_bias):
    global _NC_CACHE
    x = np.ascontiguousarray(L0_attn_hidden_states)

    def blk(w, nb):
        ofd, f = w.shape
        d = ofd // nb
        wt = np.ascontiguousarray(w.T)
        out = np.empty((nb * f, d), np.float32)
        for b_ in range(nb):
            out[b_ * f:(b_ + 1) * f, :] = wt[:, b_ * d:(b_ + 1) * d]
        return out

    wqb = blk(L0_attn_Wq, 32)
    wkb = blk(L0_attn_Wk, 8)
    wvb = blk(L0_attn_Wv, 8)
    wgb = blk(L0_attn_Wg, 32)
    wob = blk(L0_attn_Wo, 32)
    winT = np.ascontiguousarray(L0_attn_Win.T)
    fwT = np.ascontiguousarray(L0_ffn_weight.T)

    invf = (1.0 / 10000.0 ** (np.arange(0, D, 2, dtype=np.float32) / np.float32(D))).astype(np.float32)
    posL = np.arange(L, dtype=np.float32)
    ang = posL[:, None] * invf[None, :]
    cosL = np.concatenate([np.cos(ang), np.cos(ang)], axis=1).astype(np.float32)
    sinL = np.concatenate([np.sin(ang), np.sin(ang)], axis=1).astype(np.float32)
    sgn = np.ones(D, np.float32)
    sgn[:64] = -1.0
    sin2L = sinL * sgn[None, :]

    consts = _consts()
    if _NC_CACHE is None:
        _NC_CACHE = build_nc()
    nc = _NC_CACHE

    in_maps = []
    for c in range(8):
        b_ = c // 4
        lc = c % 4
        t0 = lc * TOK
        msk = np.broadcast_to((np.arange(4) < lc).astype(np.float32), (CH, 4)).copy()
        im = dict(
            x=np.ascontiguousarray(x[b_, t0:t0 + TOK]),
            wqb=wqb, wkb=wkb, wvb=wvb, wgb=wgb, winT=winT, wob=wob, fwT=fwT,
            fwsl=np.ascontiguousarray(fwT[512 * c:512 * (c + 1), :]),
            dtb=L0_attn_dt_bias.reshape(32, 1).astype(np.float32),
            alog=L0_attn_A_log_bias.reshape(32, 1).astype(np.float32),
            gnormc=L0_attn_gnorm_w.reshape(D, 1).astype(np.float32),
            fbias=np.ascontiguousarray(L0_ffn_bias.reshape(32, D).T).astype(np.float32),
            cosfm=np.ascontiguousarray(cosL[t0:t0 + TOK].T).astype(NPBF),
            sin2fm=np.ascontiguousarray(sin2L[t0:t0 + TOK].T).astype(NPBF),
            mskd=msk, ommd=np.ascontiguousarray(1.0 - msk),
            **consts,
        )
        in_maps.append(im)

    res = run_bass_kernel_spmd(nc, in_maps, core_ids=list(range(8)))
    out = np.empty((B, L, HID), np.float32)
    for c in range(8):
        out[c // 4, (c % 4) * TOK:(c % 4 + 1) * TOK] = res.results[c]["y"]
    return out
